# version 03
# baseline (speedup 1.0000x reference)
import numpy as np
import ml_dtypes
from contextlib import ExitStack

import concourse.bass as bass
import concourse.mybir as mybir
from concourse.bass_utils import run_bass_kernel_spmd

F32 = mybir.dt.float32
BF16 = mybir.dt.bfloat16
AF = mybir.ActivationFunctionType
ALU = mybir.AluOpType
AX = mybir.AxisListType
BF = ml_dtypes.bfloat16
NEG = -1.0e30

T = 4096
TH = 2048
NB = TH // 128
EM = 1024
H = 16
NEXP = 64
CS = 16


def pack_w(W):
    K, O = W.shape
    kc = K // 128
    return np.ascontiguousarray(
        W.reshape(kc, 128, O).transpose(1, 0, 2).reshape(128, kc * O).astype(np.float32))

def pack_w64(W):
    K, O = W.shape
    kc = K // 64
    return np.ascontiguousarray(
        W.reshape(kc, 64, O).transpose(1, 0, 2).reshape(64, kc * O).astype(np.float32))

def blockify(X):
    Tn, C = X.shape
    nb = Tn // 128
    return np.ascontiguousarray(X.reshape(nb, 128, C).transpose(1, 0, 2).reshape(128, nb * C))

def unblockify(Y, C):
    nb = Y.shape[1] // C
    return np.ascontiguousarray(Y.reshape(128, nb, C).transpose(1, 0, 2).reshape(nb * 128, C))

def host_consts():
    c = {}
    c["ident"] = np.eye(128, dtype=BF)
    c["identf"] = np.eye(128, dtype=np.float32)
    kp = np.arange(128)[:, None]; qc = np.arange(512)[None, :]
    mq = np.stack([np.where(qc >= kp + 128 * j, 0.0, NEG) for j in range(4)], 1)
    c["maskqk"] = np.ascontiguousarray(mq.reshape(128, 4 * 512)).astype(BF)
    qp = np.arange(128)[:, None]; kcc = np.arange(512)[None, :]
    mx = np.stack([np.where(kcc <= qp + 128 * j, 0.0, NEG) for j in range(4)], 1)
    c["maskmx"] = np.ascontiguousarray(mx.reshape(128, 4 * 512)).astype(BF)
    inv = 1.0 / 10000.0 ** (np.arange(16, dtype=np.float64) / 16.0)
    ang = np.arange(T, dtype=np.float64)[:, None] * inv[None, :]
    c["cqt"] = blockify(np.concatenate([np.cos(ang), np.cos(ang)], 1).astype(np.float32))
    c["sqt"] = blockify(np.concatenate([-np.sin(ang), np.sin(ang)], 1).astype(np.float32))
    sc96 = 1.0 / np.sqrt(96.0)
    c["cqtq"] = (c["cqt"] * sc96).astype(np.float32)
    c["sqtq"] = (c["sqt"] * sc96).astype(np.float32)
    tloc = np.arange(TH)
    in128 = (tloc % CS != 0).astype(np.float32)[None, :].repeat(128, 0)
    c["in128"] = np.ascontiguousarray(in128)
    c["in16"] = np.ascontiguousarray(in128[:16])
    c["st16"] = np.ascontiguousarray(1.0 - in128[:16])
    bm = np.zeros((16, 8, 128), dtype=np.float32)
    for hp in range(8):
        for ci in range(128):
            bm[2 * hp + ci // 64, hp, ci] = 1.0
    c["bm16"] = np.ascontiguousarray(bm.reshape(16, 8 * 128))
    c["onesc"] = np.ones((1, 128), dtype=np.float32)
    c["ones64"] = np.ones((1, 64), dtype=np.float32)
    c["ones65"] = np.ones((65, 64), dtype=np.float32)
    c["onescol"] = np.ones((128, 1), dtype=np.float32)
    return c


def build_nc(nexp=NEXP, do_a=True, do_b=True, do_c=True, do_d=True, stop_after=None, dbg=0, noswitch=False):
    nc = bass.Bass("TRN2", target_bir_lowering=False)

    def din(name, shape, dt=F32):
        return nc.dram_tensor(name, shape, dt, kind="ExternalInput").ap()

    xb_d = din("xb", [T, EM])
    wq_d = din("wq", [128, 8 * 1024]); wqr_d = din("wqr", [128, 8 * 512])
    wkr_d = din("wkr", [128, 8 * 32]); wkvd_d = din("wkvd", [128, 8 * 256])
    wku_d = din("wku", [128, 2 * 1024]); wvu_d = din("wvu", [128, 2 * 1024])
    wo64_d = din("wo64", [64, 16 * 1024]); lobo_d = din("lobo", [1, 16])
    win_d = din("win", [128, 8 * 1024]); wgate_d = din("wgate", [128, 8 * 32])
    wout_d = din("wout", [128, 8 * 1024])
    lng_d = din("lng", [128, 8]); lnb_d = din("lnb", [128, 8])
    bgate_d = din("bgate", [16, 2]); eigr_d = din("eigr", [16, 1])
    wrt_d = din("wrt", [128, 8 * 64])
    sgw_d = din("sgw", [128, 8 * 512]); suw_d = din("suw", [128, 8 * 512])
    sdw_d = din("sdw", [128, 4 * 1024])
    weg_d = din("weg", [nexp, 128, 8 * 512]); weu_d = din("weu", [nexp, 128, 8 * 512])
    wed_d = din("wed", [nexp, 128, 4 * 1024])
    flags_d = din("flags", [128, 2])
    ident_d = din("ident", [128, 128], BF16)
    identf_d = din("identf", [128, 128])
    maskqk_d = din("maskqk", [128, 4 * 512], BF16)
    maskmx_d = din("maskmx", [128, 4 * 512], BF16)
    cqt_d = din("cqt", [128, 32 * 32]); sqt_d = din("sqt", [128, 32 * 32])
    cqtq_d = din("cqtq", [128, 32 * 32]); sqtq_d = din("sqtq", [128, 32 * 32])
    in128_d = din("in128", [128, TH]); in16_d = din("in16", [16, TH])
    st16_d = din("st16", [16, TH]); bm16_d = din("bm16", [16, 8 * 128])
    onesc_d = din("onesc", [1, 128]); ones64_d = din("ones64", [1, 64])
    ones65_d = din("ones65", [65, 64])
    onescol_d = din("onescol", [128, 1])
    out_d = nc.dram_tensor("out", [128, NB * EM], F32, kind="ExternalOutput").ap()
    snd_d = nc.dram_tensor("snd", [128, 16], F32).ap()
    rcv_d = nc.dram_tensor("rcv", [128, 16], F32).ap()

    te, sc, ve, sy, gp = nc.tensor, nc.scalar, nc.vector, nc.sync, nc.gpsimd
    ENG = [te, sc, ve, sy, gp]

    es = ExitStack()
    semh, semc = {}, {}

    def sem(name):
        if name not in semh:
            semh[name] = es.enter_context(nc.semaphore(name))
            semc[name] = 0
        return semh[name]

    def inc(ins, name, n=1):
        ins.then_inc(sem(name), n); semc[name] += n

    def incd(ins, name):
        inc(ins, name, 16)

    def incd_outg(ins):
        inc(ins, "outg", 16)

    def wat(eng, name, v=None):
        s = sem(name)
        eng.wait_ge(s, semc[name] if v is None else v)

    SP = [""]

    def vsync():
        inc(ve.tensor_copy(smll[:, 28:29], smll[:, 29:30]), SP[0] + "vv", 1)
        wat(ve, SP[0] + "vv")

    def ssync(ins):
        inc(ins, SP[0] + "ssy", 1)
        wat(sc, SP[0] + "ssy")

    def vheal():
        ve.tensor_tensor(smll[:, 20:28], identfS[:, 0:8], identfS[:, 8:16], ALU.mult)
        vsync()

    def barrier():
        s = sem("bar")
        for eng in ENG:
            x = eng.nop(); x.then_inc(s, 1)
        semc["bar"] += 5
        for eng in ENG:
            eng.wait_ge(s, semc["bar"])

    with es:
        blkctx = es.enter_context(nc.Block())

        R = es.enter_context(nc.sbuf_tensor("R", [128, NB * EM], F32))
        identS = es.enter_context(nc.sbuf_tensor("identS", [128, 128], BF16))
        identfS = es.enter_context(nc.sbuf_tensor("identfS", [128, 128], F32))
        onescS = es.enter_context(nc.sbuf_tensor("onescS", [1, 128], F32))
        ones64S = es.enter_context(nc.sbuf_tensor("ones64S", [1, 64], F32))
        ones65S = es.enter_context(nc.sbuf_tensor("ones65S", [65, 64], F32))
        onescolS = es.enter_context(nc.sbuf_tensor("onescolS", [128, 1], F32))
        smll = es.enter_context(nc.sbuf_tensor("smll", [128, 32], F32))
        lnstat = es.enter_context(nc.sbuf_tensor("lnstat", [128, 96], F32))
        lnscr = es.enter_context(nc.sbuf_tensor("lnscr", [128, 2048], F32))
        actc = es.enter_context(nc.sbuf_tensor("actc", [128, 2], F32))
        loboS = es.enter_context(nc.sbuf_tensor("loboS", [1, 16], F32))
        flagS = es.enter_context(nc.sbuf_tensor("flagS", [128, 2], F32))

        ve.memset(smll[:, :], 0.0)
        ve.memset(lnstat[:, :], 0.0)
        ve.memset(actc[:, 0:1], 1e-5)
        inc(ve.memset(actc[:, 1:2], -0.5), "cl", 1)
        incd(sy.dma_start(out=identS[:, :], in_=ident_d), "cl")
        incd(sy.dma_start(out=identfS[:, :], in_=identf_d), "cl")
        incd(sy.dma_start(out=onescS[:, :], in_=onesc_d), "cl")
        incd(sy.dma_start(out=ones64S[:, :], in_=ones64_d), "cl")
        incd(sy.dma_start(out=ones65S[:, :], in_=ones65_d), "cl")
        incd(sy.dma_start(out=onescolS[:, :], in_=onescol_d), "cl")
        incd(sy.dma_start(out=loboS[:, :], in_=lobo_d), "cl")
        incd(sy.dma_start(out=flagS[:, :], in_=flags_d), "cl")

        def emit_ln_block(src, dst):
            wat(ve, SP[0] + "Ltp")
            wat(sc, SP[0] + "Lln")
            inc(ve.reduce_sum(lnstat[:, 0:1], src, axis=AX.X), SP[0] + "Lv", 1)
            inc(sc.activation(out=lnscr[:, 0:1024], in_=src, func=AF.Square,
                              accum_out=lnstat[:, 1:2]), SP[0] + "Ls", 1)
            wat(ve, SP[0] + "Lv"); wat(ve, SP[0] + "Ls")
            ve.tensor_scalar_mul(lnstat[:, 3:4], lnstat[:, 0:1], 1.0 / EM)
            vsync()
            ve.tensor_tensor(lnstat[:, 4:5], lnstat[:, 3:4], lnstat[:, 3:4], ALU.mult)
            ve.tensor_scalar_mul(lnstat[:, 5:6], lnstat[:, 1:2], 1.0 / EM)
            vsync()
            inc(ve.tensor_sub(lnstat[:, 5:6], lnstat[:, 5:6], lnstat[:, 4:5]),
                SP[0] + "Lv2", 1)
            wat(sc, SP[0] + "Lv2")
            ssync(sc.activation(out=lnstat[:, 6:7], in_=lnstat[:, 5:6], func=AF.Ln,
                                bias=actc[:, 0:1]))
            inc(sc.activation(out=lnstat[:, 6:7], in_=lnstat[:, 6:7], func=AF.Exp,
                              scale=actc[:, 1:2]), SP[0] + "Ls2", 1)
            wat(ve, SP[0] + "Ls2")
            inc(ve.scalar_tensor_tensor(lnstat[:, 7:8], lnstat[:, 3:4], -1.0,
                                        lnstat[:, 6:7], ALU.mult, ALU.mult), SP[0] + "Lv3", 1)
            wat(sc, SP[0] + "Lv3")
            inc(sc.activation(out=lnscr[:, 0:1024], in_=src, func=AF.Copy,
                              scale=lnstat[:, 6:7]), SP[0] + "Lcp", 1)
            wat(ve, SP[0] + "Lcp")
            inc(ve.tensor_scalar_add(dst, lnscr[:, 0:1024], lnstat[:, 7:8]),
                SP[0] + "Lln", 1)

        def emit_ln_R_to_xTb(pT, xlnb, xTb):
            b_tp = semc.get("Ltp", 0)
            b_ln = semc.get("Lln", 0)
            wat(ve, "Ltp"); wat(sc, "Lln")
            for qb in range(NB):
                inc(ve.reduce_sum(lnstat[:, qb:qb + 1],
                                  R[:, qb * EM:(qb + 1) * EM], axis=AX.X), "Lv", 1)
                inc(sc.activation(out=lnscr[:, 0:1024], in_=R[:, qb * EM:(qb + 1) * EM],
                                  func=AF.Square, accum_out=lnstat[:, 16 + qb:17 + qb]),
                    "Ls", 1)
            wat(ve, "Lv"); wat(ve, "Ls")
            ve.tensor_scalar_mul(lnstat[:, 32:48], lnstat[:, 0:16], 1.0 / EM)
            ve.tensor_scalar_mul(lnstat[:, 48:64], lnstat[:, 16:32], 1.0 / EM)
            vsync()
            ve.tensor_tensor(lnstat[:, 0:16], lnstat[:, 32:48], lnstat[:, 32:48], ALU.mult)
            vsync()
            inc(ve.tensor_sub(lnstat[:, 48:64], lnstat[:, 48:64], lnstat[:, 0:16]), "Lv2", 1)
            wat(sc, "Lv2")
            ssync(sc.activation(out=lnstat[:, 64:80], in_=lnstat[:, 48:64], func=AF.Ln,
                                bias=actc[:, 0:1]))
            inc(sc.activation(out=lnstat[:, 64:80], in_=lnstat[:, 64:80], func=AF.Exp,
                              scale=actc[:, 1:2]), "Ls2", 1)
            wat(ve, "Ls2")
            inc(ve.scalar_tensor_tensor(lnstat[:, 32:48], lnstat[:, 32:48], -1.0,
                                        lnstat[:, 64:80], ALU.mult, ALU.mult), "Lv3", 1)
            wat(sc, "Lv3")
            for qb in range(NB):
                pb = qb % 2
                if qb >= 2:
                    wat(sc, "Lln", b_ln + qb - 1)
                inc(sc.activation(out=lnscr[:, pb * 1024:(pb + 1) * 1024],
                                  in_=R[:, qb * EM:(qb + 1) * EM], func=AF.Copy,
                                  scale=lnstat[:, 64 + qb:65 + qb]), "Lcp", 1)
                wat(ve, "Lcp")
                if qb >= 2:
                    wat(ve, "Ltp", b_tp + 8 * (qb - 1))
                inc(ve.tensor_scalar_add(xlnb[:, pb * 1024:(pb + 1) * 1024],
                                         lnscr[:, pb * 1024:(pb + 1) * 1024],
                                         lnstat[:, 32 + qb:33 + qb]), "Lln", 1)
                for cc in range(8):
                    wat(te, "Lln")
                    if semc["Ltp"] - b_tp >= 2:
                        wat(te, "Ltv", semc["Ltv"] - 1)
                    inc(te.transpose(pT[:, (cc % 2) * 1024:(cc % 2) * 1024 + 128],
                                     xlnb[:, pb * 1024 + cc * 128: pb * 1024 + (cc + 1) * 128],
                                     identS[:, :]), "Ltp", 1)
                    wat(ve, "Ltp")
                    inc(ve.tensor_copy(xTb[:, cc * TH + qb * 128: cc * TH + (qb + 1) * 128],
                                       pT[:, (cc % 2) * 1024:(cc % 2) * 1024 + 128]),
                        "Ltv", 1)

        def emit_ffn(tag):
            p = tag
            esf = ExitStack()
            with esf:
                S2 = lambda n, s, d=F32: esf.enter_context(nc.sbuf_tensor(tag + n, s, d))
                P2 = lambda n, s, d=F32: esf.enter_context(nc.psum_tensor(tag + n, s, d))
                xTb = S2("xTb", [128, 8 * TH], BF16)
                msb = S2("msb", [128, NB * 64], F32)
                es1 = ExitStack()
                with es1:
                    pT = es1.enter_context(nc.psum_tensor(tag + "pT", [128, 2048], BF16))
                    pR = es1.enter_context(nc.psum_tensor(tag + "pR", [128, 1024], F32))
                    xlnb = es1.enter_context(nc.sbuf_tensor(tag + "xlnb", [128, 2048], BF16))
                    aff = es1.enter_context(nc.sbuf_tensor(tag + "aff", [128, 64], F32))
                    m8 = es1.enter_context(nc.sbuf_tensor(tag + "m8", [128, 16], F32))
                    wrtS = es1.enter_context(nc.sbuf_tensor(tag + "wrtS", [128, 8 * 64], BF16))
                    incd(gp.dma_start(out=wrtS[:, :], in_=wrt_d), tag + "wr")
                    emit_ln_R_to_xTb(pT, xlnb, xTb)
                    wat(te, tag + "wr")
                    wat(te, "Ltv")
                    for qb in range(NB):
                        if qb >= 2:
                            wat(te, "Fsg", semc["Fsg"] - 1)
                        for hc in range(8):
                            ins = te.matmul(pR[:, (qb % 2) * 512:(qb % 2) * 512 + 64],
                                            xTb[:, hc * TH + qb * 128: hc * TH + (qb + 1) * 128],
                                            wrtS[:, hc * 64:(hc + 1) * 64],
                                            start=(hc == 0), stop=(hc == 7))
                        inc(ins, "Frt", 1)
                        wat(sc, "Frt")
                        if qb >= 1:
                            wat(sc, "Fmk", semc["Fmk"])
                        inc(sc.activation(out=aff[:, :], in_=pR[:, (qb % 2) * 512:(qb % 2) * 512 + 64],
                                          func=AF.Sigmoid),
                            "Fsg", 1)
                        wat(ve, "Fsg")
                        ve.max(m8[:, 0:8], aff[:, :])
                        inc(ve.tensor_copy(smll[:, 28:29], smll[:, 29:30]), "Fm8", 1)
                        wat(ve, "Fm8")
                        ve.tensor_single_scalar(msb[:, qb * 64:(qb + 1) * 64], aff[:, :],
                                                m8[:, 7:8], ALU.is_ge)
                        vsync()
                        ve.tensor_tensor(msb[:, qb * 64:(qb + 1) * 64],
                                         msb[:, qb * 64:(qb + 1) * 64], aff[:, :], ALU.mult)
                        ve.reduce_sum(m8[:, 8:9], m8[:, 0:8], axis=AX.X)
                        vsync()
                        ve.tensor_scalar_add(m8[:, 8:9], m8[:, 8:9], 1e-9)
                        vsync()
                        ve.reciprocal(m8[:, 9:10], m8[:, 8:9])
                        inc(ve.tensor_copy(smll[:, 28:29], smll[:, 29:30]), "Frc", 1)
                        wat(ve, "Frc")
                        inc(ve.tensor_scalar_mul(msb[:, qb * 64:(qb + 1) * 64],
                                                 msb[:, qb * 64:(qb + 1) * 64],
                                                 m8[:, 9:10]), "Fmk", 1)
                    ve.memset(smll[:, 18:19], 1.0)
                    inc(ve.tensor_copy(smll[:, 28:29], smll[:, 29:30]), "Fmk", 1)
                    if stop_after == "ln":
                        wat(gp, "Ltv")
                        incd(gp.dma_start(out=out_d, in_=xTb[:, 0:8 * TH]), "out")
                        wat(gp, "out")
                        return
                    if stop_after == "router":
                        wat(gp, "Fmk")
                        incd(gp.dma_start(out=out_d[:, 0:NB * 64], in_=msb[:, :]), "out")
                        wat(gp, "out")
                        return
                barrier()
                wg = [S2("wg0", [128, 8 * 512], BF16), S2("wg1", [128, 8 * 512], BF16)]
                wu = [S2("wu0", [128, 8 * 512], BF16), S2("wu1", [128, 8 * 512], BF16)]
                wd = [S2("wd0", [128, 4 * 1024], BF16), S2("wd1", [128, 4 * 1024], BF16)]
                sg = S2("sg", [128, 4 * 256], BF16)
                actm = [S2("am0", [128, 4 * 256], BF16), S2("am1", [128, 4 * 256], BF16)]
                psg = P2("psg", [128, 1024]); psu = P2("psu", [128, 1024])
                psd = P2("psd", [128, 2048])
                NE1 = nexp + 1
                NT = 8
                NIDX = NE1 * NT
                FB = {k: semc.get("F" + k, 0)
                      for k in ("w", "ed", "p1", "p2s", "p2v", "p3", "p4")}
                for e in range(NE1):
                    if e >= 2:
                        wat(gp, "Fed", FB["ed"] + e - 1)
                    b = e % 2
                    srcs = (sgw_d, suw_d, sdw_d) if e == 0 else \
                           (weg_d[e - 1], weu_d[e - 1], wed_d[e - 1])
                    incd(gp.dma_start(out=wg[b][:, :], in_=srcs[0]), tag + "w")
                    incd(gp.dma_start(out=wu[b][:, :], in_=srcs[1]), tag + "w")
                    incd(gp.dma_start(out=wd[b][:, :], in_=srcs[2]), tag + "w")
                for idx in range(NIDX + 1):
                    if idx < NIDX:
                        e, t = divmod(idx, NT)
                        b = e % 2
                        if t == 0:
                            wat(te, tag + "w", FB["w"] + 48 * (e + 1))
                        if idx >= 1:
                            wat(te, "Fp2v", FB["p2v"] + idx)
                        for s in range(4):
                            for hc in range(8):
                                te.matmul(psg[:, s * 256:(s + 1) * 256],
                                          wg[b][:, hc * 512 + s * 128: hc * 512 + (s + 1) * 128],
                                          xTb[:, hc * TH + t * 256: hc * TH + (t + 1) * 256],
                                          start=(hc == 0), stop=(hc == 7))
                        for s in range(4):
                            for hc in range(8):
                                ins = te.matmul(psu[:, s * 256:(s + 1) * 256],
                                          wu[b][:, hc * 512 + s * 128: hc * 512 + (s + 1) * 128],
                                          xTb[:, hc * TH + t * 256: hc * TH + (t + 1) * 256],
                                          start=(hc == 0), stop=(hc == 7))
                        inc(ins, "Fp1", 1)
                    if idx >= 1:
                        jdx = idx - 1
                        e, t = divmod(jdx, NT)
                        b = e % 2
                        wat(te, "Fp2v", FB["p2v"] + jdx + 1)
                        if jdx >= 1:
                            wat(te, "Fp4", FB["p4"] + jdx)
                        for ic in range(4):
                            for tsub in range(2):
                                for hh in range(2):
                                    ins = te.matmul(
                                        psd[:, tsub * 1024 + hh * 512: tsub * 1024 + (hh + 1) * 512],
                                        actm[t % 2][:, ic * 256 + tsub * 128: ic * 256 + (tsub + 1) * 128],
                                        wd[b][:, ic * 1024 + hh * 512: ic * 1024 + (hh + 1) * 512],
                                        start=(ic == 0), stop=(ic == 3), skip_group_check=True)
                        inc(ins, "Fp3", 1)
                        if t == NT - 1:
                            inc(te.nop(), "Fed", 1)
                for idx in range(NIDX):
                    wat(sc, "Fp1", FB["p1"] + idx + 1)
                    for s in range(4):
                        ins = sc.activation(out=sg[:, s * 256:(s + 1) * 256],
                                            in_=psg[:, s * 256:(s + 1) * 256], func=AF.Silu)
                    inc(ins, "Fp2s", 1)
                for idx in range(NIDX + 1):
                    if idx < NIDX:
                        e, t = divmod(idx, NT)
                        wat(ve, "Fp2s", FB["p2s"] + idx + 1)
                        for s in range(4):
                            ins = ve.tensor_tensor(actm[t % 2][:, s * 256:(s + 1) * 256],
                                                   sg[:, s * 256:(s + 1) * 256],
                                                   psu[:, s * 256:(s + 1) * 256], ALU.mult)
                        inc(ins, "Fp2v", 1)
                    if idx >= 1:
                        jdx = idx - 1
                        e, t = divmod(jdx, NT)
                        wat(ve, "Fp3", FB["p3"] + jdx + 1)
                        for tsub in range(2):
                            tb = 2 * t + tsub
                            mc = smll[:, 18:19] if e == 0 else \
                                 msb[:, tb * 64 + (e - 1): tb * 64 + e]
                            for hh in range(2):
                                ins = ve.scalar_tensor_tensor(
                                    R[:, tb * EM + hh * 512: tb * EM + (hh + 1) * 512],
                                    psd[:, tsub * 1024 + hh * 512: tsub * 1024 + (hh + 1) * 512],
                                    mc,
                                    R[:, tb * EM + hh * 512: tb * EM + (hh + 1) * 512],
                                    ALU.mult, ALU.add)
                        inc(ins, "Fp4", 1)
            barrier()

        def emit_attn(par, q0, nk):
            p = "a"
            tn = "a%d" % par
            NKB = nk // 128
            esa = ExitStack()
            with esa:
                S2 = lambda n, s, d=F32: esa.enter_context(nc.sbuf_tensor(tn + n, s, d))
                latT = S2("latT", [128, 2 * 4096], BF16)
                krT = S2("krT", [32, 4096], BF16)
                xTown = S2("xTown", [128, 8 * TH], BF16)
                marena = S2("marena", [128, 16])
                cemS = S2("cemS", [65, 512])
                dbcS = S2("dbcS", [64, 512])
                drS = S2("drS", [65, 512])
                maskqkS = S2("mqk", [128, 4 * 512], BF16)
                maskmxS = S2("mmx", [128, 4 * 512], BF16)
                wkuS = S2("wkuS", [128, 2 * 1024], BF16)
                wvuS = S2("wvuS", [128, 2 * 1024], BF16)
                incd(gp.dma_start(out=wkuS[:, :], in_=wku_d), "awg")
                incd(gp.dma_start(out=wvuS[:, :], in_=wvu_d), "awg")
                incd(sy.dma_start(out=maskqkS[:, :], in_=maskqk_d), "aws")
                incd(sy.dma_start(out=maskmxS[:, :], in_=maskmx_d), "aws")
                with nc.psum_tensor(tn + "pM0", [128, 16], F32) as pM0:
                    wat(te, "cl")
                    inc(te.matmul(pM0[:, 0:16], onescS[:, :], loboS[:, :],
                                  start=True, stop=True), "alb", 1)
                    wat(ve, "alb")
                    inc(ve.tensor_copy(smll[:, 0:16], pM0[:, 0:16]), "albv", 1)

                def rope_tiles(psrc, nh, blkg, dst, cqtS, sqtS, tmr):
                    cq = cqtS[:, blkg * 32:(blkg + 1) * 32]
                    sq = sqtS[:, blkg * 32:(blkg + 1) * 32]
                    w = nh * 32
                    v3 = lambda ap: ap.rearrange("a (h f) -> a h f", f=32)
                    cqb = cq.rearrange("a (r f) -> a r f", r=1).broadcast_to((128, nh, 32))
                    ve.tensor_tensor(v3(tmr[:, 0:w]), v3(psrc), cqb, ALU.mult)
                    h16 = lambda ap, off: ap.rearrange("a (h f) -> a h f", f=16)[:, off::2, :]
                    sqb1 = sq[:, 0:16].rearrange("a (r f) -> a r f", r=1).broadcast_to((128, nh, 16))
                    sqb2 = sq[:, 16:32].rearrange("a (r f) -> a r f", r=1).broadcast_to((128, nh, 16))
                    ve.tensor_tensor(h16(tmr[:, 512:512 + w], 0), h16(psrc, 1), sqb1, ALU.mult)
                    ve.tensor_tensor(h16(tmr[:, 512:512 + w], 1), h16(psrc, 0), sqb2, ALU.mult)
                    vsync()
                    return ve.tensor_add(dst, tmr[:, 0:w], tmr[:, 512:512 + w])

                esa1 = ExitStack()
                with esa1:
                    S3 = lambda n, s, d=F32: esa1.enter_context(nc.sbuf_tensor(tn + n, s, d))
                    P3 = lambda n, s, d=F32: esa1.enter_context(nc.psum_tensor(tn + n, s, d))
                    xstage = S3("xst", [128, 4 * 1024])
                    xTsl = S3("xTsl", [128, 2 * 8 * 512], BF16)
                    xlnb2 = S3("xlnb2", [128, 1024], BF16)
                    cqtS = S3("cqtS", [128, 1024]); sqtS = S3("sqtS", [128, 1024])
                    tmr = S3("tmr", [128, 1024])
                    ropeb = S3("ropeb", [128, 512], BF16)
                    wkvdS = S3("wkvdS", [128, 8 * 256], BF16)
                    wkrS = S3("wkrS", [128, 8 * 32], BF16)
                    pT2 = P3("pT2", [128, 2048], BF16)
                    pLat = P3("pLat", [128, 1024])
                    pKR = P3("pKR", [128, 512])
                    incd(gp.dma_start(out=wkvdS[:, :], in_=wkvd_d), "awg")
                    incd(gp.dma_start(out=wkrS[:, :], in_=wkr_d), "awg")
                    incd(sy.dma_start(out=cqtS[:, :], in_=cqt_d), "aws")
                    incd(sy.dma_start(out=sqtS[:, :], in_=sqt_d), "aws")
                    wat(ve, "awg"); wat(ve, "aws")
                    wat(te, "awg"); wat(te, "aws")
                    NSL = nk // 512
                    for sl in range(NSL):
                        own = (q0 <= sl * 512 < q0 + TH)
                        ql = sl * 512 - q0
                        par2 = sl % 2
                        if own:
                            xt = xTown[:, :]
                            xtc = lambda cc, lo, w: xTown[:, cc * TH + ql + lo: cc * TH + ql + lo + w]
                        else:
                            xt = xTsl[:, par2 * 4096:(par2 + 1) * 4096]
                            xtc = lambda cc, lo, w: xt[:, cc * 512 + lo: cc * 512 + lo + w]
                        if sl >= 1:
                            wat(sy, "asl", sl)
                        for bl in range(4):
                            blkg = sl * 4 + bl
                            incd(sy.dma_start(out=xstage[:, bl * 1024:(bl + 1) * 1024],
                                              in_=xb_d[blkg * 128:(blkg + 1) * 128, :]), "ax")
                        for bl in range(4):
                            blkg = sl * 4 + bl
                            wat(ve, "ax"); wat(sc, "ax")
                            emit_ln_block(xstage[:, bl * 1024:(bl + 1) * 1024], xlnb2[:, :])
                            for cc in range(8):
                                wat(te, SP[0] + "Lln")
                                if semc.get("at2", 0) >= 2:
                                    wat(te, "at2v", semc["at2v"] - 1)
                                inc(te.transpose(pT2[:, (cc % 2) * 1024:(cc % 2) * 1024 + 128],
                                                 xlnb2[:, cc * 128:(cc + 1) * 128], identS[:, :]),
                                    "at2", 1)
                                wat(ve, "at2")
                                inc(ve.tensor_copy(xtc(cc, bl * 128, 128),
                                                   pT2[:, (cc % 2) * 1024:(cc % 2) * 1024 + 128]),
                                    "at2v", 1)
                        wat(te, "at2v")
                        for ck in range(2):
                            for hc in range(8):
                                ins = te.matmul(pLat[:, ck * 512:(ck + 1) * 512],
                                                wkvdS[:, hc * 256 + ck * 128: hc * 256 + (ck + 1) * 128],
                                                xtc(hc, 0, 512),
                                                start=(hc == 0), stop=(hc == 7))
                            inc(ins, "alt", 1)
                            wat(ve, "alt")
                            inc(ve.tensor_copy(
                                latT[:, ck * 4096 + sl * 512: ck * 4096 + (sl + 1) * 512],
                                pLat[:, ck * 512:(ck + 1) * 512]), "altv", 1)
                        for bl in range(4):
                            blkg = sl * 4 + bl
                            if semc.get("akr", 0) >= 1:
                                wat(te, "akrv", semc["akrv"])
                            for hc in range(8):
                                ins = te.matmul(pKR[:, 0:32],
                                                xtc(hc, bl * 128, 128),
                                                wkrS[:, hc * 32:(hc + 1) * 32],
                                                start=(hc == 0), stop=(hc == 7))
                            inc(ins, "akr", 1)
                            wat(ve, "akr")
                            inc(rope_tiles(pKR[:, 0:32], 1, blkg, ropeb[:, 0:32],
                                           cqtS, sqtS, tmr), "akrv", 1)
                            wat(te, "akrv")
                            wat(te, "at2v")
                            inc(te.transpose(pT2[0:32, 0:128], ropeb[:, 0:32], identS[:, :]),
                                "at2", 1)
                            wat(ve, "at2")
                            inc(ve.tensor_copy(krT[0:32, blkg * 128:(blkg + 1) * 128],
                                               pT2[0:32, 0:128]), "at2v", 1)
                            wat(te, "at2v")
                        inc(te.nop(), "asl", 1)
                    for nm in ("altv", "at2v"):
                        if nm in semc:
                            wat(te, nm)
                    if stop_after == "a1":
                        wat(gp, "altv"); wat(gp, "at2v")
                        incd_outg(gp.dma_start(out=out_d[:, 0:8192], in_=latT[:, :]))
                        incd_outg(gp.dma_start(out=out_d[0:32, 8192:12288], in_=krT[:, :]))
                        incd_outg(gp.dma_start(out=out_d[:, 12288:16384], in_=xTown[:, 0:4096]))
                        wat(gp, "outg")
                        return
                    barrier()
                wat(te, "awg"); wat(te, "aws"); wat(ve, "awg"); wat(ve, "aws"); wat(sc, "awg"); wat(sc, "aws")
                NQT = TH // 512
                for g in range(8):
                    if stop_after in ("aqp", "akv", "aat") and g >= 1:
                        break
                    esg = ExitStack()
                    with esg:
                        gn = tn + "g%d" % g
                        Sg = lambda n, s, d=F32: esg.enter_context(nc.sbuf_tensor(gn + n, s, d))
                        Qar = Sg("Qar", [128, 2 * TH], BF16)
                        yTar = Sg("yTar", [64, 2 * TH], BF16)
                        esq = ExitStack()
                        with esq:
                            Sq = lambda n, s, d=F32: esq.enter_context(nc.sbuf_tensor(gn + n, s, d))
                            cqtS = Sq("cqtS2", [128, 1024]); sqtS = Sq("sqtS2", [128, 1024])
                            tmr = Sq("tmr2", [128, 1024])
                            ropeb = Sq("ropeb2", [128, 128], BF16)
                            wqgS = Sq("wqgS", [128, 8 * 128], BF16)
                            wqrgS = Sq("wqrgS", [128, 8 * 64], BF16)
                            pQC = esq.enter_context(nc.psum_tensor(gn + "pQC", [128, 1024], F32))
                            pQR = esq.enter_context(nc.psum_tensor(gn + "pQR", [128, 512], F32))
                            pT3 = esq.enter_context(nc.psum_tensor(gn + "pT3", [128, 512], F32))
                            incd(sy.dma_start(out=cqtS[:, :], in_=cqtq_d), "aws")
                            incd(sy.dma_start(out=sqtS[:, :], in_=sqtq_d), "aws")
                            if not (dbg & 64):
                                wq3 = wq_d.rearrange("q (c o) -> q c o", o=1024)[:, :, 2 * g * 64: 2 * g * 64 + 128]
                                incd(gp.dma_start(out=wqgS.ap().rearrange("q (c o) -> q c o", o=128),
                                                  in_=wq3), "awg")
                                wqr3 = wqr_d.rearrange("q (c o) -> q c o", o=512)[:, :, 2 * g * 32: 2 * g * 32 + 64]
                                incd(gp.dma_start(out=wqrgS.ap().rearrange("q (c o) -> q c o", o=64),
                                                  in_=wqr3), "awg")
                            else:
                                incd(gp.dma_start(out=wqgS[:, 0:1024], in_=wq_d[:, 0:1024]), "awg")
                                incd(gp.dma_start(out=wqrgS[:, 0:512], in_=wqr_d[:, 0:512]), "awg")
                            wat(te, "awg"); wat(te, "aws"); wat(ve, "awg"); wat(ve, "aws")
                            inc(ve.memset(Qar[32:64, :], 0.0), "az", 1)
                            for hl in range(2):
                                for qt in range(NQT):
                                    if semc.get("abld", 0) >= 2:
                                        wat(te, "abldv", semc["abldv"] - 1)
                                    for hc in range(8):
                                        ins = te.matmul(
                                            pQC[64:128, (qt % 2) * 512:(qt % 2) * 512 + 512],
                                            wqgS[:, hc * 128 + hl * 64: hc * 128 + (hl + 1) * 64],
                                            xTown[:, hc * TH + qt * 512:hc * TH + (qt + 1) * 512],
                                            start=(hc == 0), stop=(hc == 7),
                                            tile_position=(0, 64))
                                    inc(ins, "abld", 1)
                                    wat(ve, "abld")
                                    inc(ve.tensor_scalar_mul(
                                        Qar[64:128, hl * TH + qt * 512: hl * TH + (qt + 1) * 512],
                                        pQC[64:128, (qt % 2) * 512:(qt % 2) * 512 + 512],
                                        1.0 / float(np.sqrt(96.0))),
                                        "abldv", 1)
                            for qb in ([] if dbg & 2 else range(NB)):
                                blkg = (q0 // 128) + qb
                                if semc.get("aqr", 0) >= 1:
                                    wat(te, "aqrv", semc["aqrv"])
                                for hc in range(8):
                                    ins = te.matmul(pQR[:, 0:64],
                                                    xTown[:, hc * TH + qb * 128: hc * TH + (qb + 1) * 128],
                                                    wqrgS[:, hc * 64:(hc + 1) * 64],
                                                    start=(hc == 0), stop=(hc == 7))
                                inc(ins, "aqr", 1)
                                wat(ve, "aqr")
                                if dbg & 8:
                                    inc(ve.tensor_copy(smll[:, 28:29], smll[:, 29:30]), "aqrv", 1)
                                else:
                                    inc(rope_tiles(pQR[:, 0:64], 2, blkg, ropeb[:, 0:64],
                                                   cqtS, sqtS, tmr), "aqrv", 1)
                                for hl in range(2):
                                    wat(te, "aqrv")
                                    wat(te, "at2v")
                                    inc(te.matmul(pT3[0:32, 0:128],
                                                  ropeb[:, hl * 32:(hl + 1) * 32],
                                                  identS[:, :], start=True, stop=True),
                                        "at2", 1)
                                    wat(ve, "at2")
                                    inc(ve.tensor_copy(
                                        Qar[0:32, hl * TH + qb * 128: hl * TH + (qb + 1) * 128],
                                        pT3[0:32, 0:128]), "at2v", 1)
                            for nm in ("abldv", "at2v"):
                                wat(te, nm)
                            if stop_after == "aqp" and g == 0:
                                wat(gp, "abldv"); wat(gp, "at2v")
                                incd_outg(gp.dma_start(out=out_d[:, 0:4096], in_=Qar[:, :]))
                                wat(gp, "outg")
                        barrier()
                        if stop_after == "aqp":
                            continue
                        esa2 = ExitStack()
                        with esa2:
                            S4 = lambda n, s, d=F32: esa2.enter_context(nc.sbuf_tensor(gn + n, s, d))
                            Kar = S4("Kar", [128, 2 * 4096], BF16)
                            Var = S4("Var", [128, 32 * 130], BF16)
                            Ebuf = S4("Ebuf", [128, 2 * 512], BF16)
                            pE = esa2.enter_context(nc.psum_tensor(gn + "pE", [128, 1024], F32))
                            pU = esa2.enter_context(nc.psum_tensor(gn + "pU", [128, 1024], F32))
                            pMX = esa2.enter_context(nc.psum_tensor(gn + "pMX", [128, 1024], F32))
                            pMT = esa2.enter_context(nc.psum_tensor(gn + "pMT", [128, 512], F32))
                            pBC = esa2.enter_context(nc.psum_tensor(gn + "pBC", [128, 512], F32))
                            pend = [None]
                            itc = [0]

                            def flush_pending():
                                if pend[0] is None:
                                    return
                                (hl, qt, ub) = pend[0]
                                pend[0] = None
                                wat(te, "ayt")
                                wat(te, "ard")
                                inc(te.matmul(pBC[0:64, 0:512], ones65S[64:65, :],
                                              cemS[64:65, 0:512],
                                              start=True, stop=True,
                                              tile_position=(64, 0)), "abc", 1)
                                wat(ve, "abc")
                                ve.tensor_copy(dbcS[:, :], pBC[0:64, 0:512])
                                vsync()
                                inc(ve.tensor_tensor(
                                    yTar[0:64, hl * TH + qt * 512: hl * TH + (qt + 1) * 512],
                                    pU[0:64, ub * 512: ub * 512 + 512],
                                    dbcS[:, :], ALU.mult), "ayt", 1)

                            inc(ve.memset(Kar[32:64, :], 0.0), "az", 1)
                            for h2 in range(2):
                                h = 2 * g + h2
                                for kt in range(nk // 512):
                                    if semc.get("abld", 0) >= 2:
                                        wat(te, "abldv", semc["abldv"] - 1)
                                    for ck in range(2):
                                        ins = te.matmul(
                                            pE[64:128, (kt % 2) * 512:(kt % 2) * 512 + 512],
                                            wkuS[:, ck * 1024 + h * 64: ck * 1024 + (h + 1) * 64],
                                            latT[:, ck * 4096 + kt * 512: ck * 4096 + (kt + 1) * 512],
                                            start=(ck == 0), stop=(ck == 1), tile_position=(0, 64))
                                    inc(ins, "abld", 1)
                                    wat(ve, "abld")
                                    inc(ve.tensor_copy(
                                        Kar[64:128, h2 * 4096 + kt * 512: h2 * 4096 + (kt + 1) * 512],
                                        pE[64:128, (kt % 2) * 512:(kt % 2) * 512 + 512]), "abldv", 1)
                                wat(ve, "at2v", semc.get("at2v", 0))
                                inc(ve.tensor_copy(Kar[0:32, h2 * 4096: h2 * 4096 + nk],
                                                   krT[0:32, 0:nk]), "abldv", 1)
                            ve.memset(Var.ap()[:, :].rearrange("a (k t f) -> a k t f", t=2, f=65)
                                      [:, :, :, 64:65], 1.0)
                            inc(ve.tensor_copy(smll[:, 28:29], smll[:, 29:30]), "az", 1)
                            for kt in range(NKB):
                                if semc.get("abld", 0) >= 2:
                                    wat(te, "abldv", semc["abldv"] - 1)
                                for ck in range(2):
                                    ins = te.matmul(
                                        pU[:, (kt % 2) * 512:(kt % 2) * 512 + 128],
                                        latT[:, ck * 4096 + kt * 128: ck * 4096 + (kt + 1) * 128],
                                        wvuS[:, ck * 1024 + g * 128: ck * 1024 + (g + 1) * 128],
                                        start=(ck == 0), stop=(ck == 1))
                                inc(ins, "abld", 1)
                                wat(ve, "abld"); wat(ve, "az")
                                for h2 in range(2):
                                    ins = ve.tensor_copy(
                                        Var[:, kt * 130 + h2 * 65: kt * 130 + h2 * 65 + 64],
                                        pU[:, (kt % 2) * 512 + h2 * 64: (kt % 2) * 512 + (h2 + 1) * 64])
                                inc(ins, "abldv", 1)
                            wat(te, "abldv"); wat(te, "abldv")
                            wat(te, "az"); wat(te, "az")
                            if stop_after == "akv" and g == 0:
                                wat(gp, "abldv"); wat(gp, "az")
                                incd_outg(gp.dma_start(out=out_d[:, 0:8192], in_=Kar[:, :]))
                                incd_outg(gp.dma_start(out=out_d[:, 8192:8192 + 32 * 130],
                                                       in_=Var[:, :]))
                                wat(gp, "outg")
                                return
                            for h2 in range(2):
                                h = 2 * g + h2
                                hl = h2
                                for qt in range(NQT):
                                    tq0 = q0 + qt * 512
                                    kbs = tq0 // 512 + 1
                                    it = itc[0]
                                    ub = it % 2
                                    nmx = 0
                                    for qs in range(4):
                                        q0q = tq0 + qs * 128
                                        kbq = q0q // 512 + 1
                                        for kb in range(kbq):
                                            mb = nmx % 2
                                            if nmx >= 2:
                                                wat(te, "amxr", semc["amxr"] - 1)
                                            delta = q0q - kb * 512
                                            ins = te.matmul(
                                                pMX[:, mb * 512: mb * 512 + 512],
                                                Qar[0:128, hl * TH + qt * 512 + qs * 128:
                                                    hl * TH + qt * 512 + (qs + 1) * 128],
                                                Kar[0:128, h2 * 4096 + kb * 512: h2 * 4096 + (kb + 1) * 512],
                                                start=True, stop=(delta >= 512))
                                            if delta < 512:
                                                ins = te.matmul(
                                                    pMX[:, mb * 512: mb * 512 + 512],
                                                    identS[:, :],
                                                    maskmxS[:, (delta // 128) * 512:
                                                            (delta // 128 + 1) * 512],
                                                    start=False, stop=True, skip_group_check=True)
                                            inc(ins, "amx", 1)
                                            wat(ve, "amx")
                                            inc(ve.reduce_max(marena[:, kb:kb + 1],
                                                              pMX[:, mb * 512: mb * 512 + 512],
                                                              axis=AX.X), "amxr", 1)
                                            nmx += 1
                                        vsync()
                                        inc(ve.reduce_max(marena[:, 8 + qs: 9 + qs],
                                                          marena[:, 0:kbq], axis=AX.X), "amf", 1)
                                    flush_pending()
                                    for qs in range(4):
                                        wat(te, "amf")
                                        if qs == 0 and it >= 1:
                                            wat(te, "acm", semc.get("acm", 0))
                                        inc(te.matmul(pMT[64:65, qs * 128:(qs + 1) * 128],
                                                      marena[:, 8 + qs: 9 + qs], identfS[:, :],
                                                      start=True, stop=True,
                                                      tile_position=(0, 64)), "amt", 1)
                                    wat(sc, "amt"); wat(sc, "abc")
                                    inc(sc.activation(out=cemS[64:65, 0:512], in_=pMT[64:65, 0:512],
                                                      func=AF.Exp, bias=smll[64:65, h:h + 1]),
                                        "acm", 1)
                                    nkc = 4 * kbs
                                    def emit_pv(j):
                                        ebj = j % 2
                                        wat(te, "aex")
                                        if j == 0 and it >= 2:
                                            wat(te, "ayt", it - 1)
                                        inc(te.matmul(pU[0:65, ub * 512: ub * 512 + 512],
                                                      Var[:, j * 130 + h2 * 65: j * 130 + (h2 + 1) * 65],
                                                      Ebuf[:, ebj * 512:(ebj + 1) * 512],
                                                      start=(j == 0), stop=(j == nkc - 1),
                                                      skip_group_check=True),
                                            "apv", 1)

                                    for kc in range(nkc):
                                        eb = kc % 2
                                        if kc >= 2:
                                            wat(te, "aex", semc["aex"] - 1)
                                        delta2 = kc * 128 - tq0
                                        ins = te.matmul(
                                            pE[:, eb * 512: eb * 512 + 512],
                                            Kar[0:128, h2 * 4096 + kc * 128: h2 * 4096 + (kc + 1) * 128],
                                            Qar[0:128, hl * TH + qt * 512: hl * TH + (qt + 1) * 512],
                                            start=True, stop=(delta2 < 0), skip_group_check=True)
                                        if delta2 >= 0:
                                            ins = te.matmul(
                                                pE[:, eb * 512: eb * 512 + 512], identS[:, :],
                                                maskqkS[:, (delta2 // 128) * 512:
                                                        (delta2 // 128 + 1) * 512],
                                                start=False, stop=True, skip_group_check=True)
                                        inc(ins, "aqk", 1)
                                        if kc >= 1:
                                            emit_pv(kc - 1)
                                        wat(sc, "aqk")
                                        if kc >= 2:
                                            wat(sc, "apv", semc["apv"] - 1)
                                        inc(sc.activation(out=Ebuf[:, eb * 512:(eb + 1) * 512],
                                                          in_=pE[:, eb * 512: eb * 512 + 512],
                                                          func=AF.Exp), "aex", 1)
                                    emit_pv(nkc - 1)
                                    wat(ve, "acm"); wat(ve, "apv")
                                    ve.tensor_tensor(drS[64:65, 0:512],
                                                     pU[64:65, ub * 512: ub * 512 + 512],
                                                     cemS[64:65, 0:512], ALU.add)
                                    vsync()
                                    inc(ve.reciprocal(cemS[64:65, 0:512], drS[64:65, 0:512]),
                                        "ard", 1)
                                    pend[0] = (hl, qt, ub)
                                    itc[0] += 1
                            flush_pending()
                            wat(te, "ayt"); wat(sc, "ayt"); wat(sy, "ayt")
                            wat(gp, "ayt")
                            if stop_after == "aat" and g == 0:
                                incd_outg(gp.dma_start(out=out_d[0:64, 0:4096], in_=yTar[:, :]))
                                wat(gp, "outg")
                                return
                        barrier()
                        if stop_after in ("akv", "aat"):
                            continue
                        esw = ExitStack()
                        with esw:
                            wo64gS = esw.enter_context(
                                nc.sbuf_tensor(gn + "wo64gS", [64, 2 * 1024], BF16))
                            xst3 = esw.enter_context(
                                nc.sbuf_tensor(gn + "xst3", [128, 2 * 1024], F32))
                            pWO = esw.enter_context(nc.psum_tensor(gn + "pWO", [128, 2048], F32))
                            incd(gp.dma_start(out=wo64gS[:, :],
                                              in_=wo64_d[:, 2 * g * 1024:(2 * g + 2) * 1024]),
                                 "awg")
                            wat(te, "awg"); wat(te, "aws")
                            x3b = semc.get("ax", 0)
                            for qb in range(NB):
                                if g == 0:
                                    if qb >= 2:
                                        wat(sy, "ari", semc["ari"] - 1)
                                    incd(sy.dma_start(
                                        out=xst3[:, (qb % 2) * 1024:(qb % 2 + 1) * 1024],
                                        in_=xb_d[q0 + qb * 128: q0 + (qb + 1) * 128, :]),
                                        "ax")
                                if qb >= 2:
                                    wat(te, "ari", semc["ari"] - 1)
                                for hl in range(2):
                                    for eh in range(2):
                                        ins = te.matmul(
                                            pWO[:, (qb % 2) * 1024 + eh * 512:
                                                (qb % 2) * 1024 + (eh + 1) * 512],
                                            yTar[0:64, hl * TH + qb * 128: hl * TH + (qb + 1) * 128],
                                            wo64gS[0:64, hl * 1024 + eh * 512: hl * 1024 + (eh + 1) * 512],
                                            start=(hl == 0), stop=(hl == 1))
                                inc(ins, "awo", 1)
                                wat(ve, "awo")
                                if g == 0:
                                    wat(ve, "ax", x3b + 16 * (qb + 1))
                                    inc(ve.tensor_add(R[:, qb * EM:(qb + 1) * EM],
                                                      pWO[:, (qb % 2) * 1024:(qb % 2 + 1) * 1024],
                                                      xst3[:, (qb % 2) * 1024:(qb % 2 + 1) * 1024]),
                                        "ari", 1)
                                else:
                                    inc(ve.tensor_add(R[:, qb * EM:(qb + 1) * EM],
                                                      R[:, qb * EM:(qb + 1) * EM],
                                                      pWO[:, (qb % 2) * 1024:(qb % 2 + 1) * 1024]),
                                        "ari", 1)
                            wat(te, "ari")
                        barrier()
                if stop_after not in ("a1", "aqp", "akv", "aat"):
                    for eng in ENG:
                        wat(eng, "ari")

        def emit_attn_b():
            p = "c"
            esc = ExitStack()
            with esc:
                S2 = lambda n, s, d=F32: esc.enter_context(nc.sbuf_tensor(p + n, s, d))
                lngS = S2("lngS", [128, 8]); lnbS = S2("lnbS", [128, 8])
                bgateS = S2("bgateS", [16, 2]); eigS = S2("eigS", [16, 2])
                bm16S = S2("bm16S", [16, 8 * 128])
                in128S = S2("in128S", [128, TH])
                xpn = S2("xpn", [128, 8 * TH], BF16)
                hab = S2("hab", [16, 2 * TH])
                alpha = hab.ap()[0:16, 0:TH]; beta = hab.ap()[0:16, TH:2 * TH]
                in16 = in128S.ap()[0:16, :]
                incd(sy.dma_start(out=lngS[:, :], in_=lng_d), "cw")
                incd(sy.dma_start(out=lnbS[:, :], in_=lnb_d), "cw")
                incd(sy.dma_start(out=bgateS[:, :], in_=bgate_d), "cw")
                incd(sy.dma_start(out=eigS[:, 0:1], in_=eigr_d), "cw")
                incd(sy.dma_start(out=bm16S[:, :], in_=bm16_d), "cw")
                incd(sy.dma_start(out=in128S[:, :], in_=in128_d), "cw")

                es12 = ExitStack()
                with es12:
                    S3 = lambda n, s, d=F32: es12.enter_context(nc.sbuf_tensor(p + n, s, d))
                    xTb = S3("xTb", [128, 8 * TH], BF16)
                    winS = S3("winS", [128, 8 * 1024], BF16)
                    wgateS = S3("wgateS", [128, 8 * 32], BF16)
                    incd(gp.dma_start(out=winS[:, :], in_=win_d), "cwg")
                    incd(gp.dma_start(out=wgateS[:, :], in_=wgate_d), "cwg")
                    es1 = ExitStack()
                    with es1:
                        pT = es1.enter_context(nc.psum_tensor("cpT", [128, 2048], BF16))
                        xlnb = es1.enter_context(nc.sbuf_tensor("cxlnb", [128, 2048], BF16))
                        emit_ln_R_to_xTb(pT, xlnb, xTb)
                    barrier()
                    es2 = ExitStack()
                    with es2:
                        P2 = lambda n, s: es2.enter_context(nc.psum_tensor(p + n, s, F32))
                        pC1 = P2("pC1", [128, 1024])
                        pSt = P2("pSt", [128, 1024])
                        pBC2 = P2("pBC2", [128, 1024])
                        pGA = P2("pGA", [128, 1024])
                        xpsl = es2.enter_context(nc.sbuf_tensor("cxpsl", [128, 8 * 512], F32))
                        sqsl = es2.enter_context(nc.sbuf_tensor("csqsl", [128, 512], F32))
                        strow = es2.enter_context(nc.sbuf_tensor("cstrow", [1, 2 * 512], F32))
                        wat(te, "cwg"); wat(te, "cw"); wat(te, "cl")
                        for ts in range(4):
                            for mc in range(8):
                                if mc >= 2:
                                    wat(te, "cwiv", semc["cwiv"] - 1)
                                for hc in range(8):
                                    ins = te.matmul(
                                        pC1[:, (mc % 2) * 512:(mc % 2 + 1) * 512],
                                        winS[:, hc * 1024 + mc * 128: hc * 1024 + (mc + 1) * 128],
                                        xTb[:, hc * TH + ts * 512: hc * TH + (ts + 1) * 512],
                                        start=(hc == 0), stop=(hc == 7))
                                inc(ins, "cwi", 1)
                                wat(ve, "cwi")
                                inc(ve.tensor_copy(xpsl[:, mc * 512:(mc + 1) * 512],
                                                   pC1[:, (mc % 2) * 512:(mc % 2 + 1) * 512]),
                                    "cwiv", 1)
                            wat(te, "cwiv")
                            if ts >= 1:
                                wat(te, "cvr", semc["cvr"])
                            for hc in range(8):
                                inc(te.matmul(pSt[0:1, 0:512], onescolS[:, :],
                                              xpsl[:, hc * 512:(hc + 1) * 512],
                                              start=(hc == 0), stop=(hc == 7)), "cs1", 1)
                            wat(sc, "cwiv")
                            for hc in range(8):
                                if hc >= 1:
                                    wat(sc, "cs2", semc["cs2"])
                                inc(sc.activation(out=sqsl[:, :],
                                                  in_=xpsl[:, hc * 512:(hc + 1) * 512],
                                                  func=AF.Square), "csq", 1)
                                wat(te, "csq")
                                inc(te.matmul(pSt[0:1, 512:1024], onescolS[:, :], sqsl[:, :],
                                              start=(hc == 0), stop=(hc == 7)), "cs2", 1)
                            wat(ve, "cs1"); wat(ve, "cs2")
                            ve.tensor_scalar_mul(strow[0:1, 0:512], pSt[0:1, 0:512], 1.0 / EM)
                            ve.tensor_scalar_mul(strow[0:1, 512:1024], pSt[0:1, 512:1024],
                                                 1.0 / EM)
                            vsync()
                            ve.tensor_tensor(sqsl[0:1, 0:512], strow[0:1, 0:512],
                                             strow[0:1, 0:512], ALU.mult)
                            vsync()
                            inc(ve.tensor_sub(strow[0:1, 512:1024], strow[0:1, 512:1024],
                                              sqsl[0:1, 0:512]), "cvr", 1)
                            wat(sc, "cvr")
                            ssync(sc.activation(out=strow[0:1, 512:1024], in_=strow[0:1, 512:1024],
                                                func=AF.Ln, bias=actc[0:1, 0:1]))
                            inc(sc.activation(out=strow[0:1, 512:1024], in_=strow[0:1, 512:1024],
                                              func=AF.Exp, scale=actc[0:1, 1:2]), "crs", 1)
                            wat(ve, "crs")
                            inc(ve.scalar_tensor_tensor(strow[0:1, 0:512], strow[0:1, 0:512],
                                                        -1.0, strow[0:1, 512:1024],
                                                        ALU.mult, ALU.mult), "cmr", 1)
                            wat(te, "cmr")
                            if ts >= 1:
                                wat(te, "cnx", semc["cnx"])
                            te.matmul(pBC2[:, 0:512], onescS[:, :], strow[0:1, 512:1024],
                                      start=True, stop=True)
                            inc(te.matmul(pBC2[:, 512:1024], onescS[:, :], strow[0:1, 0:512],
                                          start=True, stop=True), "cbc2", 1)
                            wat(ve, "cbc2")
                            for hc in range(8):
                                ve.tensor_tensor(xpsl[:, hc * 512:(hc + 1) * 512],
                                                 xpsl[:, hc * 512:(hc + 1) * 512],
                                                 pBC2[:, 0:512], ALU.mult)
                                vsync()
                                inc(ve.tensor_tensor(xpsl[:, hc * 512:(hc + 1) * 512],
                                                     xpsl[:, hc * 512:(hc + 1) * 512],
                                                     pBC2[:, 512:1024], ALU.add), "cnx", 1)
                                wat(sc, "cnx")
                                inc(sc.activation(out=sqsl[:, 0:512],
                                                  in_=xpsl[:, hc * 512:(hc + 1) * 512],
                                                  func=AF.Copy, scale=lngS[:, hc:hc + 1]),
                                    "cxpc", 1)
                                wat(ve, "cxpc")
                                inc(ve.tensor_scalar_add(
                                    xpn[:, hc * TH + ts * 512: hc * TH + (ts + 1) * 512],
                                    sqsl[:, 0:512], lnbS[:, hc:hc + 1]), "cxp", 1)
                            wat(te, "cxp")
                            if ts >= 1:
                                wat(te, "cab", semc["cab"])
                            for hc in range(8):
                                te.matmul(pGA[0:16, 0:512], wgateS[:, hc * 32: hc * 32 + 16],
                                          xpn[:, hc * TH + ts * 512: hc * TH + (ts + 1) * 512],
                                          start=(hc == 0), stop=(hc == 7))
                            for hc in range(8):
                                ins2 = te.matmul(pGA[0:16, 512:1024],
                                                 wgateS[:, hc * 32 + 16: hc * 32 + 32],
                                                 xpn[:, hc * TH + ts * 512: hc * TH + (ts + 1) * 512],
                                                 start=(hc == 0), stop=(hc == 7))
                            inc(ins2, "cga", 1)
                            wat(sc, "cga")
                            sc.activation(out=alpha[:, ts * 512:(ts + 1) * 512],
                                          in_=pGA[0:16, 0:512], func=AF.Sigmoid,
                                          bias=bgateS[0:16, 0:1])
                            inc(sc.activation(out=beta[:, ts * 512:(ts + 1) * 512],
                                              in_=pGA[0:16, 512:1024], func=AF.Sigmoid,
                                              bias=bgateS[0:16, 1:2]), "cab", 1)
                if stop_after == "xpn":
                    wat(gp, "cxp")
                    incd_outg(gp.dma_start(out=out_d, in_=xpn[:, :]))
                    wat(gp, "outg")
                    return
                barrier()
                esL = esc.enter_context(ExitStack())
                SL = lambda n, s, d=F32: esL.enter_context(nc.sbuf_tensor(p + n, s, d))
                hcr = SL("hcr", [16, 2 * TH])
                hw = SL("hw", [16, 2 * TH])
                rbS = SL("rbS", [128, TH]); Ssc = SL("Ssc", [128, TH])
                cnar = SL("cnar", [128, 8 * 128]); enar = SL("enar", [128, 8 * 128])
                hst = SL("hst", [128, 128]); hprev = SL("hprev", [128, 128])
                hlocar = SL("hlocar", [128, 8 * 128])
                init8 = SL("init8", [128, 8]); sndS = SL("sndS", [128, 16])
                rcvS = SL("rcvS", [128, 16])
                tmpa = SL("tmpa", [128, 512]); tmpb = SL("tmpb", [128, 512])
                cum16 = hcr.ap()[0:16, 0:TH]; agb = hcr.ap()[0:16, TH:2 * TH]
                tmpg = rbS.ap()
                w0 = hw.ap()[0:16, 0:TH]; w1 = hw.ap()[0:16, TH:2 * TH]
                d0S = SL("d0S", [128, TH])
                inc(sc.activation(out=eigS[0:16, 1:2], in_=eigS[0:16, 0:1], func=AF.Tanh),
                    "ceg", 1)
                wat(ve, "ceg"); wat(ve, "cab"); wat(ve, "cw"); wat(ve, "cwg")
                inc(ve.tensor_copy(smll[:, 28:29], smll[:, 29:30]), "cegv", 1)
                wat(ve, "cegv")
                ve.tensor_scalar_mul(alpha, alpha, eigS[0:16, 1:2])
                vsync()
                ve.tensor_tensor(w0, alpha, in16, ALU.mult)
                vsync()
                ve.tensor_sub(w1, alpha, w0)
                vheal()
                ve.tensor_tensor_scan(cum16, w0, w1, 0.0, ALU.mult, ALU.add)
                vsync()
                ve.tensor_scalar(w1, in16, -1.0, 1.0, ALU.mult, ALU.add)
                ve.tensor_tensor(tmpg[0:16, 1:TH], cum16[:, 0:TH - 1], in16[:, 1:TH], ALU.mult)
                ve.memset(tmpg[0:16, 0:1], 0.0)
                vsync()
                ve.tensor_add(w1, w1, tmpg[0:16, 0:TH])
                vsync()
                ve.tensor_single_scalar(w1, w1, 1e-8, ALU.is_gt)
                ve.tensor_tensor(agb, alpha, beta, ALU.mult)
                vsync()
                inc(ve.tensor_tensor(agb, agb, w1, ALU.mult), "chl", 1)

                st15 = lambda ap: ap.rearrange("a (n f) -> a n f", f=16)[:, :, 15:16].rearrange(
                    "a n f -> a (n f)")
                st14 = lambda ap: ap.rearrange("a (n f) -> a n f", f=16)[:, :, 14:15].rearrange(
                    "a n f -> a (n f)")

                es3 = ExitStack()
                with es3:
                    pB3 = es3.enter_context(nc.psum_tensor("cpB3", [128, 1536], F32))

                    def emit_rb_scan(hp):
                        wat(te, "cha"); wat(te, "cenv"); wat(te, "csn")
                        for tt in range(4):
                            cols = slice(tt * 512, (tt + 1) * 512)
                            wat(te, "chl")
                            if semc.get("cb3", 0) >= 1:
                                wat(te, "crb", semc["crb"])
                            te.matmul(pB3[:, 1024:1536],
                                      bm16S[:, hp * 128:(hp + 1) * 128],
                                      agb[:, cols], start=True, stop=True)
                            inc(te.matmul(pB3[:, 512:1024],
                                          bm16S[:, hp * 128:(hp + 1) * 128],
                                          w0[:, cols], start=True, stop=True), "cb3", 1)
                            wat(ve, "cb3")
                            ve.tensor_tensor(rbS[:, cols],
                                             xpn[:, hp * TH + tt * 512: hp * TH + (tt + 1) * 512],
                                             pB3[:, 1024:1536], ALU.mult)
                            inc(ve.tensor_copy(d0S[:, cols], pB3[:, 512:1024]), "crb", 1)
                        vheal()
                        inc(ve.tensor_tensor_scan(Ssc[:, :], d0S[:, :], rbS[:, :],
                                                  0.0, ALU.mult, ALU.add), "css", 1)

                    for hp in range(8):
                        ccols = slice(hp * 128, (hp + 1) * 128)
                        emit_rb_scan(hp)
                        wat(te, "css")
                        if semc.get("ccnv", 0) >= 1:
                            wat(te, "cenv", semc["cenv"])
                        te.matmul(pB3[:, 0:128], bm16S[:, hp * 128:(hp + 1) * 128],
                                  st15(cum16), start=True, stop=True)
                        inc(te.matmul(pB3[:, 128:256], bm16S[:, hp * 128:(hp + 1) * 128],
                                      st15(beta), start=True, stop=True), "ccb", 1)
                        wat(ve, "ccb")
                        inc(ve.tensor_copy(cnar.ap()[:, ccols], pB3[:, 0:128]), "ccnv", 1)
                        vsync()
                        ve.tensor_tensor(enar.ap()[:, ccols], st15(d0S.ap()[:, :]),
                                         st14(Ssc.ap()[:, :]), ALU.mult)
                        ve.tensor_tensor(tmpa[:, 0:128],
                                         st15(xpn.ap()[:, hp * TH:(hp + 1) * TH]),
                                         pB3[:, 128:256], ALU.mult)
                        vsync()
                        inc(ve.tensor_add(enar.ap()[:, ccols], enar.ap()[:, ccols],
                                          tmpa[:, 0:128]), "cenv", 1)
                        vheal()
                        inc(ve.tensor_tensor_scan(hlocar.ap()[:, ccols], cnar.ap()[:, ccols],
                                                  enar.ap()[:, ccols], 0.0,
                                                  ALU.mult, ALU.add), "csn", 1)
                    wat(ve, "csn"); wat(ve, "cl")
                    ve.tensor_scalar(smll[:, 17:18], flagS[:, 0:1], -1.0, 1.0, ALU.mult, ALU.add)
                    vheal()
                    ve.tensor_scalar_mul(sndS[:, 8:16], hlocar.ap().rearrange("a (n f) -> a n f", f=128)[:, :, 127:128].rearrange("a n f -> a (n f)"), flagS[:, 0:1])
                    vsync()
                    inc(ve.tensor_scalar_mul(sndS[:, 0:8], hlocar.ap().rearrange("a (n f) -> a n f", f=128)[:, :, 127:128].rearrange("a n f -> a (n f)"), smll[:, 17:18]),
                        "csf", 1)
                    wat(gp, "csf")
                    incd(gp.dma_start(out=snd_d, in_=sndS[:, :]), "csd")
                    wat(gp, "csd")
                    inc(gp.collective_compute("AllReduce", ALU.add,
                                              replica_groups=[[0, 1], [2, 3], [4, 5], [6, 7]],
                                              ins=[snd_d.opt()], outs=[rcv_d.opt()]), "ccc", 1)
                    wat(gp, "ccc")
                    incd(gp.dma_start(out=rcvS[:, :], in_=rcv_d), "crv")
                    wat(ve, "crv")
                    inc(ve.tensor_scalar_mul(init8[:, :], rcvS[:, 0:8], flagS[:, 0:1]),
                        "ci8", 1)
                    if stop_after == "exch":
                        wat(gp, "ci8")
                        incd_outg(gp.dma_start(out=out_d[:, 0:16], in_=sndS[:, :]))
                        incd_outg(gp.dma_start(out=out_d[:, 16:32], in_=rcvS[:, :]))
                        incd_outg(gp.dma_start(out=out_d[:, 32:40], in_=init8[:, :]))
                        incd_outg(gp.dma_start(out=out_d[:, 64:64 + 1024], in_=cnar[:, :]))
                        incd_outg(gp.dma_start(out=out_d[:, 2048:2048 + 1024], in_=enar[:, :]))
                        incd_outg(gp.dma_start(out=out_d[:, 3200:3200 + 128], in_=hst[:, :]))
                        wat(gp, "outg")
                        return
                    for hp in range(8):
                        ccols = slice(hp * 128, (hp + 1) * 128)
                        wat(ve, "ci8")
                        vheal()
                        ve.tensor_tensor_scan(hst[:, :], cnar.ap()[:, ccols],
                                              enar.ap()[:, ccols], init8[:, hp:hp + 1],
                                              ALU.mult, ALU.add)
                        vsync()
                        ve.tensor_copy(hprev[:, 1:128], hst[:, 0:127])
                        inc(ve.tensor_copy(hprev[:, 0:1], init8[:, hp:hp + 1]), "chp", 1)
                        emit_rb_scan(hp)
                        wat(ve, "css")
                        for tt in range(4):
                            cols = slice(tt * 512, (tt + 1) * 512)
                            wat(te, "chp")
                            if semc.get("cb4", 0) >= 1:
                                wat(te, "cha", semc["cha"])
                            te.matmul(pB3[:, 0:512], bm16S[:, hp * 128:(hp + 1) * 128],
                                      cum16[:, cols], start=True, stop=True)
                            inc(te.matmul(pB3[:, 512:1024], bm16S[:, hp * 128:(hp + 1) * 128],
                                          beta[:, cols], start=True, stop=True), "cb4", 1)
                            wat(ve, "cb4")
                            if tt == 0:
                                ve.memset(tmpa[:, 0:1], 0.0)
                                ve.tensor_tensor(tmpa[:, 1:512], Ssc[:, 0:511],
                                                 d0S[:, 1:512], ALU.mult)
                            else:
                                ve.tensor_tensor(tmpa[:, 0:512],
                                                 Ssc[:, tt * 512 - 1:(tt + 1) * 512 - 1],
                                                 d0S[:, tt * 512:(tt + 1) * 512], ALU.mult)
                            hpb = hprev[:, tt * 32:(tt + 1) * 32].rearrange(
                                "a (n r) -> a n r", r=1).broadcast_to((128, 32, 16))
                            ve.tensor_tensor(tmpb[:, 0:512].rearrange("a (n f) -> a n f", f=16),
                                             pB3[:, 0:512].rearrange("a (n f) -> a n f", f=16),
                                             hpb, ALU.mult)
                            vsync()
                            ve.tensor_add(tmpa[:, 0:512], tmpa[:, 0:512], tmpb[:, 0:512])
                            ve.tensor_tensor(tmpb[:, 0:512],
                                             xpn[:, hp * TH + tt * 512: hp * TH + (tt + 1) * 512],
                                             pB3[:, 512:1024], ALU.mult)
                            vsync()
                            inc(ve.tensor_add(
                                xpn[:, hp * TH + tt * 512: hp * TH + (tt + 1) * 512],
                                tmpa[:, 0:512], tmpb[:, 0:512]), "cha", 1)
                barrier()
                esL.close()
                es4 = ExitStack()
                with es4:
                    woutS = es4.enter_context(nc.sbuf_tensor("cwoutS", [128, 8 * 1024], BF16))
                    pWo = es4.enter_context(nc.psum_tensor("cpWo", [128, 2048], F32))
                    incd(gp.dma_start(out=woutS[:, :], in_=wout_d), "cwg4")
                    wat(te, "cwg4")
                    for qb in range(NB):
                        if qb >= 2:
                            wat(te, "cra", semc["cra"] - 1)
                        for hp in range(8):
                            for eh in range(2):
                                ins = te.matmul(
                                    pWo[:, (qb % 2) * 1024 + eh * 512:(qb % 2) * 1024 + (eh + 1) * 512],
                                    xpn[:, hp * TH + qb * 128: hp * TH + (qb + 1) * 128],
                                    woutS[:, hp * 1024 + eh * 512: hp * 1024 + (eh + 1) * 512],
                                    start=(hp == 0), stop=(hp == 7))
                        inc(ins, "cwo2", 1)
                        wat(ve, "cwo2")
                        inc(ve.tensor_add(R[:, qb * EM:(qb + 1) * EM],
                                          R[:, qb * EM:(qb + 1) * EM],
                                          pWo[:, (qb % 2) * 1024:(qb % 2 + 1) * 1024]),
                            "cra", 1)
            barrier()

        for eng in ENG:
            wat(eng, "cl")
        if do_a and noswitch:
            emit_attn(0, 0, TH)
            if stop_after in ("a1", "aqp", "akv", "aat"):
                return nc
            barrier()
        elif do_a:
            snap = dict(semc)
            idxs = [eng.partition_id() % 2 for eng in ENG]
            CROSS = ("vv", "ssy", "Lln", "Lv", "Ls", "Lv2", "Ls2", "Lv3",
                     "Lcp", "Ltp", "Ltv", "bar")
            cnt = [None, None]
            for i in nc.Switch(engines=ENG, index=idxs, n=2):
                for k in list(semc.keys()):
                    semc[k] = snap.get(k, 0)
                if i == 0:
                    emit_attn(0, 0, TH)
                else:
                    emit_attn(1, TH, T)
                cnt[i] = {n: semc.get(n, 0) for n in CROSS}
            if stop_after in ("a1", "aqp", "akv", "aat"):
                return nc
            for i in nc.Switch(engines=ENG, index=idxs, n=2):
                if i == 0:
                    for n in CROSS:
                        delta = cnt[1][n] - cnt[0][n]
                        assert delta >= 0, (n, delta)
                        for _ in range(delta):
                            te.nop().then_inc(sem(n), 1)
                else:
                    te.nop()
            barrier()
        else:
            for qb in range(NB):
                incd(sy.dma_start(out=R[:, qb * EM:(qb + 1) * EM],
                                  in_=xb_d[qb * 128:(qb + 1) * 128, :]), "r0")
            for eng in ENG:
                wat(eng, "r0")
            barrier()
        if do_b:
            emit_ffn("B")
            if stop_after in ("ln", "router"):
                return nc
        if do_c:
            emit_attn_b()
            if stop_after in ("xpn", "exch"):
                return nc
        if do_d:
            emit_ffn("D")
        incd(sy.dma_start(out=out_d, in_=R[:, :]), "out")
        wat(sy, "out")
    return nc


_CACHE = {}

def kernel(**inputs):
    ii = {k: np.asarray(v) for k, v in inputs.items()}
    x = ii["L0_attn_a_x"].astype(np.float32)
    consts = host_consts()
    shared = {
        "wq": pack_w(ii["L0_attn_a_Wq"]),
        "wqr": pack_w(ii["L0_attn_a_Wqr"]),
        "wkr": pack_w(ii["L0_attn_a_Wkr"]),
        "wkvd": pack_w(ii["L0_attn_a_Wkvd"]),
        "wku": pack_w(ii["L0_attn_a_Wku"]),
        "wvu": pack_w(ii["L0_attn_a_Wvu"]),
        "wo64": pack_w64(ii["L0_attn_a_Wo"]),
        "lobo": ii["L0_attn_a_lobo_log"].reshape(1, 16).astype(np.float32),
        "win": pack_w(np.ascontiguousarray(ii["L0_attn_b_W_in"].T)),
        "wgate": pack_w(np.ascontiguousarray(ii["L0_attn_b_W_gate"].T)),
        "wout": pack_w(np.ascontiguousarray(ii["L0_attn_b_W_out"].T)),
        "lng": np.ascontiguousarray(ii["L0_attn_b_ln_g"].reshape(8, 128).T.astype(np.float32)),
        "lnb": np.ascontiguousarray(ii["L0_attn_b_ln_b"].reshape(8, 128).T.astype(np.float32)),
        "bgate": np.ascontiguousarray(ii["L0_attn_b_b_gate"].reshape(2, 16).T.astype(np.float32)),
        "eigr": ii["L0_attn_b_eig_raw"].reshape(16, 1).astype(np.float32),
        "wrt": pack_w(ii["L0_ffn_router_w"]),
        "sgw": pack_w(ii["L0_ffn_shared_gate_w"]),
        "suw": pack_w(ii["L0_ffn_shared_up_w"]),
        "sdw": pack_w(ii["L0_ffn_shared_down_w"]),
        "weg": np.stack([pack_w(ii["L0_ffn_expert_gate_k"][e]) for e in range(NEXP)]),
        "weu": np.stack([pack_w(ii["L0_ffn_expert_up_k"][e]) for e in range(NEXP)]),
        "wed": np.stack([pack_w(ii["L0_ffn_expert_down_k"][e]) for e in range(NEXP)]),
    }
    shared.update(consts)
    if "nc" not in _CACHE:
        _CACHE["nc"] = build_nc()
    nc = _CACHE["nc"]
    in_maps = []
    for c in range(8):
        b, half = c // 2, c % 2
        m = dict(shared)
        m["xb"] = np.ascontiguousarray(x[b])
        fl = np.zeros((128, 2), np.float32)
        fl[:, 0] = float(half)
        m["flags"] = fl
        in_maps.append(m)
    res = run_bass_kernel_spmd(nc, in_maps, core_ids=list(range(8)))
    out = np.empty((4, 4096, 1024), np.float32)
    for c in range(8):
        b, half = c // 2, c % 2
        out[b, half * TH:(half + 1) * TH, :] = unblockify(res.results[c]["out"], EM)
    return out


# version 04
# speedup vs baseline: 1.0536x; 1.0536x over previous
import numpy as np
import ml_dtypes
from contextlib import ExitStack

import concourse.bass as bass
import concourse.mybir as mybir
from concourse.bass_utils import run_bass_kernel_spmd

F32 = mybir.dt.float32
BF16 = mybir.dt.bfloat16
AF = mybir.ActivationFunctionType
ALU = mybir.AluOpType
AX = mybir.AxisListType
BF = ml_dtypes.bfloat16
NEG = -1.0e30

T = 4096
TH = 2048
NB = TH // 128
EM = 1024
H = 16
NEXP = 64
CS = 16


def pack_w(W):
    K, O = W.shape
    kc = K // 128
    return np.ascontiguousarray(
        W.reshape(kc, 128, O).transpose(1, 0, 2).reshape(128, kc * O).astype(np.float32))

def pack_w64(W):
    K, O = W.shape
    kc = K // 64
    return np.ascontiguousarray(
        W.reshape(kc, 64, O).transpose(1, 0, 2).reshape(64, kc * O).astype(np.float32))

def blockify(X):
    Tn, C = X.shape
    nb = Tn // 128
    return np.ascontiguousarray(X.reshape(nb, 128, C).transpose(1, 0, 2).reshape(128, nb * C))

def unblockify(Y, C):
    nb = Y.shape[1] // C
    return np.ascontiguousarray(Y.reshape(128, nb, C).transpose(1, 0, 2).reshape(nb * 128, C))

def host_consts():
    c = {}
    c["ident"] = np.eye(128, dtype=BF)
    c["identf"] = np.eye(128, dtype=np.float32)
    kp = np.arange(128)[:, None]; qc = np.arange(512)[None, :]
    mq = np.stack([np.where(qc >= kp + 128 * j, 0.0, NEG) for j in range(4)], 1)
    c["maskqk"] = np.ascontiguousarray(mq.reshape(128, 4 * 512)).astype(BF)
    qp = np.arange(128)[:, None]; kcc = np.arange(512)[None, :]
    mx = np.stack([np.where(kcc <= qp + 128 * j, 0.0, NEG) for j in range(4)], 1)
    c["maskmx"] = np.ascontiguousarray(mx.reshape(128, 4 * 512)).astype(BF)
    inv = 1.0 / 10000.0 ** (np.arange(16, dtype=np.float64) / 16.0)
    ang = np.arange(T, dtype=np.float64)[:, None] * inv[None, :]
    c["cqt"] = blockify(np.concatenate([np.cos(ang), np.cos(ang)], 1).astype(np.float32))
    c["sqt"] = blockify(np.concatenate([-np.sin(ang), np.sin(ang)], 1).astype(np.float32))
    sc96 = 1.0 / np.sqrt(96.0)
    c["cqtq"] = (c["cqt"] * sc96).astype(np.float32)
    c["sqtq"] = (c["sqt"] * sc96).astype(np.float32)
    tloc = np.arange(TH)
    in128 = (tloc % CS != 0).astype(np.float32)[None, :].repeat(128, 0)
    c["in128"] = np.ascontiguousarray(in128)
    c["in16"] = np.ascontiguousarray(in128[:16])
    c["st16"] = np.ascontiguousarray(1.0 - in128[:16])
    bm = np.zeros((16, 8, 128), dtype=np.float32)
    for hp in range(8):
        for ci in range(128):
            bm[2 * hp + ci // 64, hp, ci] = 1.0
    c["bm16"] = np.ascontiguousarray(bm.reshape(16, 8 * 128))
    c["onesc"] = np.ones((1, 128), dtype=np.float32)
    c["ones64"] = np.ones((1, 64), dtype=np.float32)
    c["ones65"] = np.ones((65, 64), dtype=np.float32)
    c["onescol"] = np.ones((128, 1), dtype=np.float32)
    return c


def build_nc(nexp=NEXP, do_a=True, do_b=True, do_c=True, do_d=True, stop_after=None, dbg=0, noswitch=False):
    nc = bass.Bass("TRN2", target_bir_lowering=False)

    def din(name, shape, dt=F32):
        return nc.dram_tensor(name, shape, dt, kind="ExternalInput").ap()

    xb_d = din("xb", [T, EM])
    wq_d = din("wq", [128, 8 * 1024]); wqr_d = din("wqr", [128, 8 * 512])
    wkr_d = din("wkr", [128, 8 * 32]); wkvd_d = din("wkvd", [128, 8 * 256])
    wku_d = din("wku", [128, 2 * 1024]); wvu_d = din("wvu", [128, 2 * 1024])
    wo64_d = din("wo64", [64, 16 * 1024]); lobo_d = din("lobo", [1, 16])
    win_d = din("win", [128, 8 * 1024]); wgate_d = din("wgate", [128, 8 * 32])
    wout_d = din("wout", [128, 8 * 1024])
    lng_d = din("lng", [128, 8]); lnb_d = din("lnb", [128, 8])
    bgate_d = din("bgate", [16, 2]); eigr_d = din("eigr", [16, 1])
    wrt_d = din("wrt", [128, 8 * 64])
    sgw_d = din("sgw", [128, 8 * 512]); suw_d = din("suw", [128, 8 * 512])
    sdw_d = din("sdw", [128, 4 * 1024])
    weg_d = din("weg", [nexp, 128, 8 * 512]); weu_d = din("weu", [nexp, 128, 8 * 512])
    wed_d = din("wed", [nexp, 128, 4 * 1024])
    flags_d = din("flags", [128, 2])
    ident_d = din("ident", [128, 128], BF16)
    identf_d = din("identf", [128, 128])
    maskqk_d = din("maskqk", [128, 4 * 512], BF16)
    maskmx_d = din("maskmx", [128, 4 * 512], BF16)
    cqt_d = din("cqt", [128, 32 * 32]); sqt_d = din("sqt", [128, 32 * 32])
    cqtq_d = din("cqtq", [128, 32 * 32]); sqtq_d = din("sqtq", [128, 32 * 32])
    in128_d = din("in128", [128, TH]); in16_d = din("in16", [16, TH])
    st16_d = din("st16", [16, TH]); bm16_d = din("bm16", [16, 8 * 128])
    onesc_d = din("onesc", [1, 128]); ones64_d = din("ones64", [1, 64])
    ones65_d = din("ones65", [65, 64])
    onescol_d = din("onescol", [128, 1])
    out_d = nc.dram_tensor("out", [128, NB * EM], F32, kind="ExternalOutput").ap()
    snd_d = nc.dram_tensor("snd", [128, 16], F32).ap()
    rcv_d = nc.dram_tensor("rcv", [128, 16], F32).ap()

    te, sc, ve, sy, gp = nc.tensor, nc.scalar, nc.vector, nc.sync, nc.gpsimd
    ENG = [te, sc, ve, sy, gp]

    es = ExitStack()
    semh, semc = {}, {}

    def sem(name):
        if name not in semh:
            semh[name] = es.enter_context(nc.semaphore(name))
            semc[name] = 0
        return semh[name]

    def inc(ins, name, n=1):
        ins.then_inc(sem(name), n); semc[name] += n

    def incd(ins, name):
        inc(ins, name, 16)

    def incd_outg(ins):
        inc(ins, "outg", 16)

    def wat(eng, name, v=None):
        s = sem(name)
        eng.wait_ge(s, semc[name] if v is None else v)

    SP = [""]

    def vsync():
        inc(ve.tensor_copy(smll[:, 28:29], smll[:, 29:30]), SP[0] + "vv", 1)
        wat(ve, SP[0] + "vv")

    def ssync(ins):
        inc(ins, SP[0] + "ssy", 1)
        wat(sc, SP[0] + "ssy")

    def vheal():
        ve.tensor_tensor(smll[:, 20:28], identfS[:, 0:8], identfS[:, 8:16], ALU.mult)
        vsync()

    def barrier():
        s = sem("bar")
        for eng in ENG:
            x = eng.nop(); x.then_inc(s, 1)
        semc["bar"] += 5
        for eng in ENG:
            eng.wait_ge(s, semc["bar"])

    with es:
        blkctx = es.enter_context(nc.Block())

        R = es.enter_context(nc.sbuf_tensor("R", [128, NB * EM], F32))
        identS = es.enter_context(nc.sbuf_tensor("identS", [128, 128], BF16))
        identfS = es.enter_context(nc.sbuf_tensor("identfS", [128, 128], F32))
        onescS = es.enter_context(nc.sbuf_tensor("onescS", [1, 128], F32))
        ones64S = es.enter_context(nc.sbuf_tensor("ones64S", [1, 64], F32))
        ones65S = es.enter_context(nc.sbuf_tensor("ones65S", [65, 64], F32))
        onescolS = es.enter_context(nc.sbuf_tensor("onescolS", [128, 1], F32))
        smll = es.enter_context(nc.sbuf_tensor("smll", [128, 32], F32))
        lnstat = es.enter_context(nc.sbuf_tensor("lnstat", [128, 96], F32))
        lnscr = es.enter_context(nc.sbuf_tensor("lnscr", [128, 2048], F32))
        actc = es.enter_context(nc.sbuf_tensor("actc", [128, 2], F32))
        loboS = es.enter_context(nc.sbuf_tensor("loboS", [1, 16], F32))
        flagS = es.enter_context(nc.sbuf_tensor("flagS", [128, 2], F32))

        ve.memset(smll[:, :], 0.0)
        ve.memset(lnstat[:, :], 0.0)
        ve.memset(actc[:, 0:1], 1e-5)
        inc(ve.memset(actc[:, 1:2], -0.5), "cl", 1)
        incd(sy.dma_start(out=identS[:, :], in_=ident_d), "cl")
        incd(sy.dma_start(out=identfS[:, :], in_=identf_d), "cl")
        incd(sy.dma_start(out=onescS[:, :], in_=onesc_d), "cl")
        incd(sy.dma_start(out=ones64S[:, :], in_=ones64_d), "cl")
        incd(sy.dma_start(out=ones65S[:, :], in_=ones65_d), "cl")
        incd(sy.dma_start(out=onescolS[:, :], in_=onescol_d), "cl")
        incd(sy.dma_start(out=loboS[:, :], in_=lobo_d), "cl")
        incd(sy.dma_start(out=flagS[:, :], in_=flags_d), "cl")

        def emit_ln_block(src, dst):
            wat(ve, SP[0] + "Ltp")
            wat(sc, SP[0] + "Lln")
            inc(ve.reduce_sum(lnstat[:, 0:1], src, axis=AX.X), SP[0] + "Lv", 1)
            inc(sc.activation(out=lnscr[:, 0:1024], in_=src, func=AF.Square,
                              accum_out=lnstat[:, 1:2]), SP[0] + "Ls", 1)
            wat(ve, SP[0] + "Lv"); wat(ve, SP[0] + "Ls")
            ve.tensor_scalar_mul(lnstat[:, 3:4], lnstat[:, 0:1], 1.0 / EM)
            vsync()
            ve.tensor_tensor(lnstat[:, 4:5], lnstat[:, 3:4], lnstat[:, 3:4], ALU.mult)
            ve.tensor_scalar_mul(lnstat[:, 5:6], lnstat[:, 1:2], 1.0 / EM)
            vsync()
            inc(ve.tensor_sub(lnstat[:, 5:6], lnstat[:, 5:6], lnstat[:, 4:5]),
                SP[0] + "Lv2", 1)
            wat(sc, SP[0] + "Lv2")
            ssync(sc.activation(out=lnstat[:, 6:7], in_=lnstat[:, 5:6], func=AF.Ln,
                                bias=actc[:, 0:1]))
            inc(sc.activation(out=lnstat[:, 6:7], in_=lnstat[:, 6:7], func=AF.Exp,
                              scale=actc[:, 1:2]), SP[0] + "Ls2", 1)
            wat(ve, SP[0] + "Ls2")
            inc(ve.scalar_tensor_tensor(lnstat[:, 7:8], lnstat[:, 3:4], -1.0,
                                        lnstat[:, 6:7], ALU.mult, ALU.mult), SP[0] + "Lv3", 1)
            wat(sc, SP[0] + "Lv3")
            inc(sc.activation(out=lnscr[:, 0:1024], in_=src, func=AF.Copy,
                              scale=lnstat[:, 6:7]), SP[0] + "Lcp", 1)
            wat(ve, SP[0] + "Lcp")
            inc(ve.tensor_scalar_add(dst, lnscr[:, 0:1024], lnstat[:, 7:8]),
                SP[0] + "Lln", 1)

        def emit_ln_R_to_xTb(pT, xlnb, xTb):
            b_tp = semc.get("Ltp", 0)
            b_ln = semc.get("Lln", 0)
            wat(ve, "Ltp"); wat(sc, "Lln")
            for qb in range(NB):
                inc(ve.reduce_sum(lnstat[:, qb:qb + 1],
                                  R[:, qb * EM:(qb + 1) * EM], axis=AX.X), "Lv", 1)
                inc(sc.activation(out=lnscr[:, 0:1024], in_=R[:, qb * EM:(qb + 1) * EM],
                                  func=AF.Square, accum_out=lnstat[:, 16 + qb:17 + qb]),
                    "Ls", 1)
            wat(ve, "Lv"); wat(ve, "Ls")
            ve.tensor_scalar_mul(lnstat[:, 32:48], lnstat[:, 0:16], 1.0 / EM)
            ve.tensor_scalar_mul(lnstat[:, 48:64], lnstat[:, 16:32], 1.0 / EM)
            vsync()
            ve.tensor_tensor(lnstat[:, 0:16], lnstat[:, 32:48], lnstat[:, 32:48], ALU.mult)
            vsync()
            inc(ve.tensor_sub(lnstat[:, 48:64], lnstat[:, 48:64], lnstat[:, 0:16]), "Lv2", 1)
            wat(sc, "Lv2")
            ssync(sc.activation(out=lnstat[:, 64:80], in_=lnstat[:, 48:64], func=AF.Ln,
                                bias=actc[:, 0:1]))
            inc(sc.activation(out=lnstat[:, 64:80], in_=lnstat[:, 64:80], func=AF.Exp,
                              scale=actc[:, 1:2]), "Ls2", 1)
            wat(ve, "Ls2")
            inc(ve.scalar_tensor_tensor(lnstat[:, 32:48], lnstat[:, 32:48], -1.0,
                                        lnstat[:, 64:80], ALU.mult, ALU.mult), "Lv3", 1)
            wat(sc, "Lv3")
            for qb in range(NB):
                pb = qb % 2
                if qb >= 2:
                    wat(sc, "Lln", b_ln + qb - 1)
                inc(sc.activation(out=lnscr[:, pb * 1024:(pb + 1) * 1024],
                                  in_=R[:, qb * EM:(qb + 1) * EM], func=AF.Copy,
                                  scale=lnstat[:, 64 + qb:65 + qb]), "Lcp", 1)
                wat(ve, "Lcp")
                if qb >= 2:
                    wat(ve, "Ltp", b_tp + 8 * (qb - 1))
                inc(ve.tensor_scalar_add(xlnb[:, pb * 1024:(pb + 1) * 1024],
                                         lnscr[:, pb * 1024:(pb + 1) * 1024],
                                         lnstat[:, 32 + qb:33 + qb]), "Lln", 1)
                for cc in range(8):
                    wat(te, "Lln")
                    if semc["Ltp"] - b_tp >= 2:
                        wat(te, "Ltv", semc["Ltv"] - 1)
                    inc(te.transpose(pT[:, (cc % 2) * 1024:(cc % 2) * 1024 + 128],
                                     xlnb[:, pb * 1024 + cc * 128: pb * 1024 + (cc + 1) * 128],
                                     identS[:, :]), "Ltp", 1)
                    wat(ve, "Ltp")
                    inc(ve.tensor_copy(xTb[:, cc * TH + qb * 128: cc * TH + (qb + 1) * 128],
                                       pT[:, (cc % 2) * 1024:(cc % 2) * 1024 + 128]),
                        "Ltv", 1)

        def emit_ffn(tag):
            p = tag
            esf = ExitStack()
            with esf:
                S2 = lambda n, s, d=F32: esf.enter_context(nc.sbuf_tensor(tag + n, s, d))
                P2 = lambda n, s, d=F32: esf.enter_context(nc.psum_tensor(tag + n, s, d))
                xTb = S2("xTb", [128, 8 * TH], BF16)
                msb = S2("msb", [128, NB * 64], F32)
                es1 = ExitStack()
                with es1:
                    pT = es1.enter_context(nc.psum_tensor(tag + "pT", [128, 2048], BF16))
                    pR = es1.enter_context(nc.psum_tensor(tag + "pR", [128, 1024], F32))
                    xlnb = es1.enter_context(nc.sbuf_tensor(tag + "xlnb", [128, 2048], BF16))
                    aff = es1.enter_context(nc.sbuf_tensor(tag + "aff", [128, 64], F32))
                    m8 = es1.enter_context(nc.sbuf_tensor(tag + "m8", [128, 16], F32))
                    wrtS = es1.enter_context(nc.sbuf_tensor(tag + "wrtS", [128, 8 * 64], BF16))
                    incd(gp.dma_start(out=wrtS[:, :], in_=wrt_d), tag + "wr")
                    emit_ln_R_to_xTb(pT, xlnb, xTb)
                    wat(te, tag + "wr")
                    wat(te, "Ltv")
                    for qb in range(NB):
                        if qb >= 2:
                            wat(te, "Fsg", semc["Fsg"] - 1)
                        for hc in range(8):
                            ins = te.matmul(pR[:, (qb % 2) * 512:(qb % 2) * 512 + 64],
                                            xTb[:, hc * TH + qb * 128: hc * TH + (qb + 1) * 128],
                                            wrtS[:, hc * 64:(hc + 1) * 64],
                                            start=(hc == 0), stop=(hc == 7))
                        inc(ins, "Frt", 1)
                        wat(sc, "Frt")
                        if qb >= 1:
                            wat(sc, "Fmk", semc["Fmk"])
                        inc(sc.activation(out=aff[:, :], in_=pR[:, (qb % 2) * 512:(qb % 2) * 512 + 64],
                                          func=AF.Sigmoid),
                            "Fsg", 1)
                        wat(ve, "Fsg")
                        ve.max(m8[:, 0:8], aff[:, :])
                        inc(ve.tensor_copy(smll[:, 28:29], smll[:, 29:30]), "Fm8", 1)
                        wat(ve, "Fm8")
                        ve.tensor_single_scalar(msb[:, qb * 64:(qb + 1) * 64], aff[:, :],
                                                m8[:, 7:8], ALU.is_ge)
                        vsync()
                        ve.tensor_tensor(msb[:, qb * 64:(qb + 1) * 64],
                                         msb[:, qb * 64:(qb + 1) * 64], aff[:, :], ALU.mult)
                        ve.reduce_sum(m8[:, 8:9], m8[:, 0:8], axis=AX.X)
                        vsync()
                        ve.tensor_scalar_add(m8[:, 8:9], m8[:, 8:9], 1e-9)
                        vsync()
                        ve.reciprocal(m8[:, 9:10], m8[:, 8:9])
                        inc(ve.tensor_copy(smll[:, 28:29], smll[:, 29:30]), "Frc", 1)
                        wat(ve, "Frc")
                        inc(ve.tensor_scalar_mul(msb[:, qb * 64:(qb + 1) * 64],
                                                 msb[:, qb * 64:(qb + 1) * 64],
                                                 m8[:, 9:10]), "Fmk", 1)
                    ve.memset(smll[:, 18:19], 1.0)
                    inc(ve.tensor_copy(smll[:, 28:29], smll[:, 29:30]), "Fmk", 1)
                    if stop_after == "ln":
                        wat(gp, "Ltv")
                        incd(gp.dma_start(out=out_d, in_=xTb[:, 0:8 * TH]), "out")
                        wat(gp, "out")
                        return
                    if stop_after == "router":
                        wat(gp, "Fmk")
                        incd(gp.dma_start(out=out_d[:, 0:NB * 64], in_=msb[:, :]), "out")
                        wat(gp, "out")
                        return
                barrier()
                wg = [S2("wg0", [128, 8 * 512], BF16), S2("wg1", [128, 8 * 512], BF16)]
                wu = [S2("wu0", [128, 8 * 512], BF16), S2("wu1", [128, 8 * 512], BF16)]
                wd = [S2("wd0", [128, 4 * 1024], BF16), S2("wd1", [128, 4 * 1024], BF16)]
                sg = S2("sg", [128, 4 * 256], BF16)
                actm = [S2("am0", [128, 4 * 256], BF16), S2("am1", [128, 4 * 256], BF16)]
                psg = P2("psg", [128, 1024]); psu = P2("psu", [128, 1024])
                psd = P2("psd", [128, 2048])
                NE1 = nexp + 1
                NT = 8
                NIDX = NE1 * NT
                FB = {k: semc.get("F" + k, 0)
                      for k in ("w", "ed", "p1", "p2s", "p2v", "p3", "p4")}
                for e in range(NE1):
                    if e >= 2:
                        wat(gp, "Fed", FB["ed"] + e - 1)
                    b = e % 2
                    srcs = (sgw_d, suw_d, sdw_d) if e == 0 else \
                           (weg_d[e - 1], weu_d[e - 1], wed_d[e - 1])
                    incd(gp.dma_start(out=wg[b][:, :], in_=srcs[0]), tag + "w")
                    incd(gp.dma_start(out=wu[b][:, :], in_=srcs[1]), tag + "w")
                    incd(gp.dma_start(out=wd[b][:, :], in_=srcs[2]), tag + "w")
                for idx in range(NIDX + 1):
                    if idx < NIDX:
                        e, t = divmod(idx, NT)
                        b = e % 2
                        if t == 0:
                            wat(te, tag + "w", FB["w"] + 48 * (e + 1))
                        if idx >= 1:
                            wat(te, "Fp2v", FB["p2v"] + idx)
                        for s in range(4):
                            for hc in range(8):
                                te.matmul(psg[:, s * 256:(s + 1) * 256],
                                          wg[b][:, hc * 512 + s * 128: hc * 512 + (s + 1) * 128],
                                          xTb[:, hc * TH + t * 256: hc * TH + (t + 1) * 256],
                                          start=(hc == 0), stop=(hc == 7))
                        for s in range(4):
                            for hc in range(8):
                                ins = te.matmul(psu[:, s * 256:(s + 1) * 256],
                                          wu[b][:, hc * 512 + s * 128: hc * 512 + (s + 1) * 128],
                                          xTb[:, hc * TH + t * 256: hc * TH + (t + 1) * 256],
                                          start=(hc == 0), stop=(hc == 7))
                        inc(ins, "Fp1", 1)
                    if idx >= 1:
                        jdx = idx - 1
                        e, t = divmod(jdx, NT)
                        b = e % 2
                        wat(te, "Fp2v", FB["p2v"] + jdx + 1)
                        if jdx >= 1:
                            wat(te, "Fp4", FB["p4"] + jdx)
                        for ic in range(4):
                            for tsub in range(2):
                                for hh in range(2):
                                    ins = te.matmul(
                                        psd[:, tsub * 1024 + hh * 512: tsub * 1024 + (hh + 1) * 512],
                                        actm[t % 2][:, ic * 256 + tsub * 128: ic * 256 + (tsub + 1) * 128],
                                        wd[b][:, ic * 1024 + hh * 512: ic * 1024 + (hh + 1) * 512],
                                        start=(ic == 0), stop=(ic == 3), skip_group_check=True)
                        inc(ins, "Fp3", 1)
                        if t == NT - 1:
                            inc(te.nop(), "Fed", 1)
                for idx in range(NIDX):
                    wat(sc, "Fp1", FB["p1"] + idx + 1)
                    for s in range(4):
                        ins = sc.activation(out=sg[:, s * 256:(s + 1) * 256],
                                            in_=psg[:, s * 256:(s + 1) * 256], func=AF.Silu)
                    inc(ins, "Fp2s", 1)
                for idx in range(NIDX + 1):
                    if idx < NIDX:
                        e, t = divmod(idx, NT)
                        wat(ve, "Fp2s", FB["p2s"] + idx + 1)
                        for s in range(4):
                            ins = ve.tensor_tensor(actm[t % 2][:, s * 256:(s + 1) * 256],
                                                   sg[:, s * 256:(s + 1) * 256],
                                                   psu[:, s * 256:(s + 1) * 256], ALU.mult)
                        inc(ins, "Fp2v", 1)
                    if idx >= 1:
                        jdx = idx - 1
                        e, t = divmod(jdx, NT)
                        wat(ve, "Fp3", FB["p3"] + jdx + 1)
                        for tsub in range(2):
                            tb = 2 * t + tsub
                            mc = smll[:, 18:19] if e == 0 else \
                                 msb[:, tb * 64 + (e - 1): tb * 64 + e]
                            for hh in range(2):
                                ins = ve.scalar_tensor_tensor(
                                    R[:, tb * EM + hh * 512: tb * EM + (hh + 1) * 512],
                                    psd[:, tsub * 1024 + hh * 512: tsub * 1024 + (hh + 1) * 512],
                                    mc,
                                    R[:, tb * EM + hh * 512: tb * EM + (hh + 1) * 512],
                                    ALU.mult, ALU.add)
                        inc(ins, "Fp4", 1)
            barrier()

        def emit_attn(par, q0, nk):
            p = "a"
            tn = "a%d" % par
            NKB = nk // 128
            esa = ExitStack()
            with esa:
                S2 = lambda n, s, d=F32: esa.enter_context(nc.sbuf_tensor(tn + n, s, d))
                latT = S2("latT", [128, 2 * 4096], BF16)
                krT = S2("krT", [32, 4096], BF16)
                xTown = S2("xTown", [128, 8 * TH], BF16)
                marena = S2("marena", [128, 16])
                cemS = S2("cemS", [65, 512])
                dbcS = S2("dbcS", [64, 512])
                drS = S2("drS", [65, 512])
                maskqkS = S2("mqk", [128, 4 * 512], BF16)
                maskmxS = S2("mmx", [128, 4 * 512], BF16)
                wkuS = S2("wkuS", [128, 2 * 1024], BF16)
                wvuS = S2("wvuS", [128, 2 * 1024], BF16)
                incd(gp.dma_start(out=wkuS[:, :], in_=wku_d), "awg")
                incd(gp.dma_start(out=wvuS[:, :], in_=wvu_d), "awg")
                incd(sy.dma_start(out=maskqkS[:, :], in_=maskqk_d), "aws")
                incd(sy.dma_start(out=maskmxS[:, :], in_=maskmx_d), "aws")
                with nc.psum_tensor(tn + "pM0", [128, 16], F32) as pM0:
                    wat(te, "cl")
                    inc(te.matmul(pM0[:, 0:16], onescS[:, :], loboS[:, :],
                                  start=True, stop=True), "alb", 1)
                    wat(ve, "alb")
                    inc(ve.tensor_copy(smll[:, 0:16], pM0[:, 0:16]), "albv", 1)

                def rope_tiles(psrc, nh, blkg, dst, cqtS, sqtS, tmr):
                    cq = cqtS[:, blkg * 32:(blkg + 1) * 32]
                    sq = sqtS[:, blkg * 32:(blkg + 1) * 32]
                    w = nh * 32
                    v3 = lambda ap: ap.rearrange("a (h f) -> a h f", f=32)
                    cqb = cq.rearrange("a (r f) -> a r f", r=1).broadcast_to((128, nh, 32))
                    ve.tensor_tensor(v3(tmr[:, 0:w]), v3(psrc), cqb, ALU.mult)
                    h16 = lambda ap, off: ap.rearrange("a (h f) -> a h f", f=16)[:, off::2, :]
                    sqb1 = sq[:, 0:16].rearrange("a (r f) -> a r f", r=1).broadcast_to((128, nh, 16))
                    sqb2 = sq[:, 16:32].rearrange("a (r f) -> a r f", r=1).broadcast_to((128, nh, 16))
                    ve.tensor_tensor(h16(tmr[:, 512:512 + w], 0), h16(psrc, 1), sqb1, ALU.mult)
                    ve.tensor_tensor(h16(tmr[:, 512:512 + w], 1), h16(psrc, 0), sqb2, ALU.mult)
                    vsync()
                    return ve.tensor_add(dst, tmr[:, 0:w], tmr[:, 512:512 + w])

                esa1 = ExitStack()
                with esa1:
                    S3 = lambda n, s, d=F32: esa1.enter_context(nc.sbuf_tensor(tn + n, s, d))
                    P3 = lambda n, s, d=F32: esa1.enter_context(nc.psum_tensor(tn + n, s, d))
                    xstage = S3("xst", [128, 4 * 1024])
                    xTsl = S3("xTsl", [128, 2 * 8 * 512], BF16)
                    xlnb2 = S3("xlnb2", [128, 2048], BF16)
                    cqtS = S3("cqtS", [128, 1024]); sqtS = S3("sqtS", [128, 1024])
                    tmr = S3("tmr", [128, 1024])
                    ropeb = S3("ropeb", [128, 512], BF16)
                    wkvdS = S3("wkvdS", [128, 8 * 256], BF16)
                    wkrS = S3("wkrS", [128, 8 * 32], BF16)
                    pT2 = P3("pT2", [128, 2048], BF16)
                    pLat = P3("pLat", [128, 1024])
                    pKR = P3("pKR", [128, 512])
                    incd(gp.dma_start(out=wkvdS[:, :], in_=wkvd_d), "awg")
                    incd(gp.dma_start(out=wkrS[:, :], in_=wkr_d), "awg")
                    incd(sy.dma_start(out=cqtS[:, :], in_=cqt_d), "aws")
                    incd(sy.dma_start(out=sqtS[:, :], in_=sqt_d), "aws")
                    wat(ve, "awg"); wat(ve, "aws")
                    wat(te, "awg"); wat(te, "aws")
                    NSL = nk // 512
                    for sl in range(NSL):
                        own = (q0 <= sl * 512 < q0 + TH)
                        ql = sl * 512 - q0
                        par2 = sl % 2
                        if own:
                            xt = xTown[:, :]
                            xtc = lambda cc, lo, w: xTown[:, cc * TH + ql + lo: cc * TH + ql + lo + w]
                        else:
                            xt = xTsl[:, par2 * 4096:(par2 + 1) * 4096]
                            xtc = lambda cc, lo, w: xt[:, cc * 512 + lo: cc * 512 + lo + w]
                        if sl >= 1:
                            wat(sy, "asl", sl)
                        for bl in range(4):
                            blkg = sl * 4 + bl
                            incd(sy.dma_start(out=xstage[:, bl * 1024:(bl + 1) * 1024],
                                              in_=xb_d[blkg * 128:(blkg + 1) * 128, :]), "ax")
                        wat(ve, "ax"); wat(sc, "ax")
                        b_ln2 = semc.get("Lln", 0)
                        b_t2 = semc.get("at2", 0)
                        for bl in range(4):
                            inc(ve.reduce_sum(lnstat[:, bl:bl + 1],
                                              xstage[:, bl * 1024:(bl + 1) * 1024],
                                              axis=AX.X), "Lv", 1)
                            inc(sc.activation(out=lnscr[:, 0:1024],
                                              in_=xstage[:, bl * 1024:(bl + 1) * 1024],
                                              func=AF.Square,
                                              accum_out=lnstat[:, 16 + bl:17 + bl]), "Ls", 1)
                        wat(ve, "Lv"); wat(ve, "Ls")
                        ve.tensor_scalar_mul(lnstat[:, 32:36], lnstat[:, 0:4], 1.0 / EM)
                        ve.tensor_scalar_mul(lnstat[:, 48:52], lnstat[:, 16:20], 1.0 / EM)
                        vsync()
                        ve.tensor_tensor(lnstat[:, 0:4], lnstat[:, 32:36], lnstat[:, 32:36],
                                         ALU.mult)
                        vsync()
                        inc(ve.tensor_sub(lnstat[:, 48:52], lnstat[:, 48:52], lnstat[:, 0:4]),
                            "Lv2", 1)
                        wat(sc, "Lv2")
                        ssync(sc.activation(out=lnstat[:, 64:68], in_=lnstat[:, 48:52],
                                            func=AF.Ln, bias=actc[:, 0:1]))
                        inc(sc.activation(out=lnstat[:, 64:68], in_=lnstat[:, 64:68],
                                          func=AF.Exp, scale=actc[:, 1:2]), "Ls2", 1)
                        wat(ve, "Ls2")
                        inc(ve.scalar_tensor_tensor(lnstat[:, 32:36], lnstat[:, 32:36], -1.0,
                                                    lnstat[:, 64:68], ALU.mult, ALU.mult),
                            "Lv3", 1)
                        wat(sc, "Lv3")
                        for bl in range(4):
                            pb = bl % 2
                            if bl >= 2:
                                wat(sc, "Lln", b_ln2 + bl - 1)
                            inc(sc.activation(out=lnscr[:, pb * 1024:(pb + 1) * 1024],
                                              in_=xstage[:, bl * 1024:(bl + 1) * 1024],
                                              func=AF.Copy,
                                              scale=lnstat[:, 64 + bl:65 + bl]), "Lcp", 1)
                            wat(ve, "Lcp")
                            if bl >= 2:
                                wat(ve, "at2", b_t2 + 8 * (bl - 1))
                            inc(ve.tensor_scalar_add(xlnb2[:, pb * 1024:(pb + 1) * 1024],
                                                     lnscr[:, pb * 1024:(pb + 1) * 1024],
                                                     lnstat[:, 32 + bl:33 + bl]), "Lln", 1)
                            for cc in range(8):
                                wat(te, "Lln")
                                if semc.get("at2", 0) >= 2:
                                    wat(te, "at2v", semc["at2v"] - 1)
                                inc(te.transpose(pT2[:, (cc % 2) * 1024:(cc % 2) * 1024 + 128],
                                                 xlnb2[:, pb * 1024 + cc * 128:
                                                        pb * 1024 + (cc + 1) * 128],
                                                 identS[:, :]),
                                    "at2", 1)
                                wat(ve, "at2")
                                inc(ve.tensor_copy(xtc(cc, bl * 128, 128),
                                                   pT2[:, (cc % 2) * 1024:(cc % 2) * 1024 + 128]),
                                    "at2v", 1)
                        wat(te, "at2v")
                        for ck in range(2):
                            for hc in range(8):
                                ins = te.matmul(pLat[:, ck * 512:(ck + 1) * 512],
                                                wkvdS[:, hc * 256 + ck * 128: hc * 256 + (ck + 1) * 128],
                                                xtc(hc, 0, 512),
                                                start=(hc == 0), stop=(hc == 7))
                            inc(ins, "alt", 1)
                            wat(ve, "alt")
                            inc(ve.tensor_copy(
                                latT[:, ck * 4096 + sl * 512: ck * 4096 + (sl + 1) * 512],
                                pLat[:, ck * 512:(ck + 1) * 512]), "altv", 1)
                        for bl in range(4):
                            blkg = sl * 4 + bl
                            if semc.get("akr", 0) >= 1:
                                wat(te, "akrv", semc["akrv"])
                            for hc in range(8):
                                ins = te.matmul(pKR[:, 0:32],
                                                xtc(hc, bl * 128, 128),
                                                wkrS[:, hc * 32:(hc + 1) * 32],
                                                start=(hc == 0), stop=(hc == 7))
                            inc(ins, "akr", 1)
                            wat(ve, "akr")
                            inc(rope_tiles(pKR[:, 0:32], 1, blkg, ropeb[:, 0:32],
                                           cqtS, sqtS, tmr), "akrv", 1)
                            wat(te, "akrv")
                            wat(te, "at2v")
                            inc(te.transpose(pT2[0:32, 0:128], ropeb[:, 0:32], identS[:, :]),
                                "at2", 1)
                            wat(ve, "at2")
                            inc(ve.tensor_copy(krT[0:32, blkg * 128:(blkg + 1) * 128],
                                               pT2[0:32, 0:128]), "at2v", 1)
                            wat(te, "at2v")
                        inc(te.nop(), "asl", 1)
                    for nm in ("altv", "at2v"):
                        if nm in semc:
                            wat(te, nm)
                    if stop_after == "a1":
                        wat(gp, "altv"); wat(gp, "at2v")
                        incd_outg(gp.dma_start(out=out_d[:, 0:8192], in_=latT[:, :]))
                        incd_outg(gp.dma_start(out=out_d[0:32, 8192:12288], in_=krT[:, :]))
                        incd_outg(gp.dma_start(out=out_d[:, 12288:16384], in_=xTown[:, 0:4096]))
                        wat(gp, "outg")
                        return
                    barrier()
                wat(te, "awg"); wat(te, "aws"); wat(ve, "awg"); wat(ve, "aws"); wat(sc, "awg"); wat(sc, "aws")
                NQT = TH // 512
                for g in range(8):
                    if stop_after in ("aqp", "akv", "aat") and g >= 1:
                        break
                    esg = ExitStack()
                    with esg:
                        gn = tn + "g%d" % g
                        Sg = lambda n, s, d=F32: esg.enter_context(nc.sbuf_tensor(gn + n, s, d))
                        Qar = Sg("Qar", [128, 2 * TH], BF16)
                        yTar = Sg("yTar", [64, 2 * TH], BF16)
                        esq = ExitStack()
                        with esq:
                            Sq = lambda n, s, d=F32: esq.enter_context(nc.sbuf_tensor(gn + n, s, d))
                            cqtS = Sq("cqtS2", [128, 1024]); sqtS = Sq("sqtS2", [128, 1024])
                            tmr = Sq("tmr2", [128, 1024])
                            ropeb = Sq("ropeb2", [128, 128], BF16)
                            wqgS = Sq("wqgS", [128, 8 * 128], BF16)
                            wqrgS = Sq("wqrgS", [128, 8 * 64], BF16)
                            pQC = esq.enter_context(nc.psum_tensor(gn + "pQC", [128, 1024], F32))
                            pQR = esq.enter_context(nc.psum_tensor(gn + "pQR", [128, 512], F32))
                            pT3 = esq.enter_context(nc.psum_tensor(gn + "pT3", [128, 512], F32))
                            incd(sy.dma_start(out=cqtS[:, :], in_=cqtq_d), "aws")
                            incd(sy.dma_start(out=sqtS[:, :], in_=sqtq_d), "aws")
                            if not (dbg & 64):
                                wq3 = wq_d.rearrange("q (c o) -> q c o", o=1024)[:, :, 2 * g * 64: 2 * g * 64 + 128]
                                incd(gp.dma_start(out=wqgS.ap().rearrange("q (c o) -> q c o", o=128),
                                                  in_=wq3), "awg")
                                wqr3 = wqr_d.rearrange("q (c o) -> q c o", o=512)[:, :, 2 * g * 32: 2 * g * 32 + 64]
                                incd(gp.dma_start(out=wqrgS.ap().rearrange("q (c o) -> q c o", o=64),
                                                  in_=wqr3), "awg")
                            else:
                                incd(gp.dma_start(out=wqgS[:, 0:1024], in_=wq_d[:, 0:1024]), "awg")
                                incd(gp.dma_start(out=wqrgS[:, 0:512], in_=wqr_d[:, 0:512]), "awg")
                            wat(te, "awg"); wat(te, "aws"); wat(ve, "awg"); wat(ve, "aws")
                            inc(ve.memset(Qar[32:64, :], 0.0), "az", 1)
                            for hl in range(2):
                                for qt in range(NQT):
                                    if semc.get("abld", 0) >= 2:
                                        wat(te, "abldv", semc["abldv"] - 1)
                                    for hc in range(8):
                                        ins = te.matmul(
                                            pQC[64:128, (qt % 2) * 512:(qt % 2) * 512 + 512],
                                            wqgS[:, hc * 128 + hl * 64: hc * 128 + (hl + 1) * 64],
                                            xTown[:, hc * TH + qt * 512:hc * TH + (qt + 1) * 512],
                                            start=(hc == 0), stop=(hc == 7),
                                            tile_position=(0, 64))
                                    inc(ins, "abld", 1)
                                    wat(ve, "abld")
                                    inc(ve.tensor_scalar_mul(
                                        Qar[64:128, hl * TH + qt * 512: hl * TH + (qt + 1) * 512],
                                        pQC[64:128, (qt % 2) * 512:(qt % 2) * 512 + 512],
                                        1.0 / float(np.sqrt(96.0))),
                                        "abldv", 1)
                            for qb in ([] if dbg & 2 else range(NB)):
                                blkg = (q0 // 128) + qb
                                if semc.get("aqr", 0) >= 1:
                                    wat(te, "aqrv", semc["aqrv"])
                                for hc in range(8):
                                    ins = te.matmul(pQR[:, 0:64],
                                                    xTown[:, hc * TH + qb * 128: hc * TH + (qb + 1) * 128],
                                                    wqrgS[:, hc * 64:(hc + 1) * 64],
                                                    start=(hc == 0), stop=(hc == 7))
                                inc(ins, "aqr", 1)
                                wat(ve, "aqr")
                                if dbg & 8:
                                    inc(ve.tensor_copy(smll[:, 28:29], smll[:, 29:30]), "aqrv", 1)
                                else:
                                    inc(rope_tiles(pQR[:, 0:64], 2, blkg, ropeb[:, 0:64],
                                                   cqtS, sqtS, tmr), "aqrv", 1)
                                for hl in range(2):
                                    wat(te, "aqrv")
                                    wat(te, "at2v")
                                    inc(te.matmul(pT3[0:32, 0:128],
                                                  ropeb[:, hl * 32:(hl + 1) * 32],
                                                  identS[:, :], start=True, stop=True),
                                        "at2", 1)
                                    wat(ve, "at2")
                                    inc(ve.tensor_copy(
                                        Qar[0:32, hl * TH + qb * 128: hl * TH + (qb + 1) * 128],
                                        pT3[0:32, 0:128]), "at2v", 1)
                            for nm in ("abldv", "at2v"):
                                wat(te, nm)
                            if stop_after == "aqp" and g == 0:
                                wat(gp, "abldv"); wat(gp, "at2v")
                                incd_outg(gp.dma_start(out=out_d[:, 0:4096], in_=Qar[:, :]))
                                wat(gp, "outg")
                        barrier()
                        if stop_after == "aqp":
                            continue
                        esa2 = ExitStack()
                        with esa2:
                            S4 = lambda n, s, d=F32: esa2.enter_context(nc.sbuf_tensor(gn + n, s, d))
                            Kar = S4("Kar", [128, 2 * 4096], BF16)
                            Var = S4("Var", [128, 32 * 130], BF16)
                            Ebuf = S4("Ebuf", [128, 2 * 512], BF16)
                            pE = esa2.enter_context(nc.psum_tensor(gn + "pE", [128, 1024], F32))
                            pU = esa2.enter_context(nc.psum_tensor(gn + "pU", [128, 1024], F32))
                            pMX = esa2.enter_context(nc.psum_tensor(gn + "pMX", [128, 1024], F32))
                            pMT = esa2.enter_context(nc.psum_tensor(gn + "pMT", [128, 512], F32))
                            pBC = esa2.enter_context(nc.psum_tensor(gn + "pBC", [128, 512], F32))
                            pend = [None]
                            itc = [0]

                            def flush_pending():
                                if pend[0] is None:
                                    return
                                (hl, qt, ub) = pend[0]
                                pend[0] = None
                                wat(te, "ayt")
                                wat(te, "ard")
                                inc(te.matmul(pBC[0:64, 0:512], ones65S[64:65, :],
                                              cemS[64:65, 0:512],
                                              start=True, stop=True,
                                              tile_position=(64, 0)), "abc", 1)
                                wat(ve, "abc")
                                ve.tensor_copy(dbcS[:, :], pBC[0:64, 0:512])
                                vsync()
                                inc(ve.tensor_tensor(
                                    yTar[0:64, hl * TH + qt * 512: hl * TH + (qt + 1) * 512],
                                    pU[0:64, ub * 512: ub * 512 + 512],
                                    dbcS[:, :], ALU.mult), "ayt", 1)

                            inc(ve.memset(Kar[32:64, :], 0.0), "az", 1)
                            for h2 in range(2):
                                h = 2 * g + h2
                                for kt in range(nk // 512):
                                    if semc.get("abld", 0) >= 2:
                                        wat(te, "abldv", semc["abldv"] - 1)
                                    for ck in range(2):
                                        ins = te.matmul(
                                            pE[64:128, (kt % 2) * 512:(kt % 2) * 512 + 512],
                                            wkuS[:, ck * 1024 + h * 64: ck * 1024 + (h + 1) * 64],
                                            latT[:, ck * 4096 + kt * 512: ck * 4096 + (kt + 1) * 512],
                                            start=(ck == 0), stop=(ck == 1), tile_position=(0, 64))
                                    inc(ins, "abld", 1)
                                    wat(ve, "abld")
                                    inc(ve.tensor_copy(
                                        Kar[64:128, h2 * 4096 + kt * 512: h2 * 4096 + (kt + 1) * 512],
                                        pE[64:128, (kt % 2) * 512:(kt % 2) * 512 + 512]), "abldv", 1)
                                wat(ve, "at2v", semc.get("at2v", 0))
                                inc(ve.tensor_copy(Kar[0:32, h2 * 4096: h2 * 4096 + nk],
                                                   krT[0:32, 0:nk]), "abldv", 1)
                            ve.memset(Var.ap()[:, :].rearrange("a (k t f) -> a k t f", t=2, f=65)
                                      [:, :, :, 64:65], 1.0)
                            inc(ve.tensor_copy(smll[:, 28:29], smll[:, 29:30]), "az", 1)
                            for kt in range(NKB):
                                if semc.get("abld", 0) >= 2:
                                    wat(te, "abldv", semc["abldv"] - 1)
                                for ck in range(2):
                                    ins = te.matmul(
                                        pU[:, (kt % 2) * 512:(kt % 2) * 512 + 128],
                                        latT[:, ck * 4096 + kt * 128: ck * 4096 + (kt + 1) * 128],
                                        wvuS[:, ck * 1024 + g * 128: ck * 1024 + (g + 1) * 128],
                                        start=(ck == 0), stop=(ck == 1))
                                inc(ins, "abld", 1)
                                wat(ve, "abld"); wat(ve, "az")
                                for h2 in range(2):
                                    ins = ve.tensor_copy(
                                        Var[:, kt * 130 + h2 * 65: kt * 130 + h2 * 65 + 64],
                                        pU[:, (kt % 2) * 512 + h2 * 64: (kt % 2) * 512 + (h2 + 1) * 64])
                                inc(ins, "abldv", 1)
                            wat(te, "abldv"); wat(te, "abldv")
                            wat(te, "az"); wat(te, "az")
                            if stop_after == "akv" and g == 0:
                                wat(gp, "abldv"); wat(gp, "az")
                                incd_outg(gp.dma_start(out=out_d[:, 0:8192], in_=Kar[:, :]))
                                incd_outg(gp.dma_start(out=out_d[:, 8192:8192 + 32 * 130],
                                                       in_=Var[:, :]))
                                wat(gp, "outg")
                                return
                            for h2 in range(2):
                                h = 2 * g + h2
                                hl = h2
                                for qt in range(NQT):
                                    tq0 = q0 + qt * 512
                                    kbs = tq0 // 512 + 1
                                    it = itc[0]
                                    ub = it % 2
                                    nmx = 0
                                    for qs in range(4):
                                        q0q = tq0 + qs * 128
                                        kbq = q0q // 512 + 1
                                        for kb in range(kbq):
                                            mb = nmx % 2
                                            if nmx >= 2:
                                                wat(te, "amxr", semc["amxr"] - 1)
                                            delta = q0q - kb * 512
                                            ins = te.matmul(
                                                pMX[:, mb * 512: mb * 512 + 512],
                                                Qar[0:128, hl * TH + qt * 512 + qs * 128:
                                                    hl * TH + qt * 512 + (qs + 1) * 128],
                                                Kar[0:128, h2 * 4096 + kb * 512: h2 * 4096 + (kb + 1) * 512],
                                                start=True, stop=(delta >= 512))
                                            if delta < 512:
                                                ins = te.matmul(
                                                    pMX[:, mb * 512: mb * 512 + 512],
                                                    identS[:, :],
                                                    maskmxS[:, (delta // 128) * 512:
                                                            (delta // 128 + 1) * 512],
                                                    start=False, stop=True, skip_group_check=True)
                                            inc(ins, "amx", 1)
                                            wat(ve, "amx")
                                            inc(ve.reduce_max(marena[:, kb:kb + 1],
                                                              pMX[:, mb * 512: mb * 512 + 512],
                                                              axis=AX.X), "amxr", 1)
                                            nmx += 1
                                        vsync()
                                        inc(ve.reduce_max(marena[:, 8 + qs: 9 + qs],
                                                          marena[:, 0:kbq], axis=AX.X), "amf", 1)
                                    flush_pending()
                                    for qs in range(4):
                                        wat(te, "amf")
                                        if qs == 0 and it >= 1:
                                            wat(te, "acm", semc.get("acm", 0))
                                        inc(te.matmul(pMT[64:65, qs * 128:(qs + 1) * 128],
                                                      marena[:, 8 + qs: 9 + qs], identfS[:, :],
                                                      start=True, stop=True,
                                                      tile_position=(0, 64)), "amt", 1)
                                    wat(sc, "amt"); wat(sc, "abc")
                                    inc(sc.activation(out=cemS[64:65, 0:512], in_=pMT[64:65, 0:512],
                                                      func=AF.Exp, bias=smll[64:65, h:h + 1]),
                                        "acm", 1)
                                    nkc = 4 * kbs
                                    def emit_pv(j):
                                        ebj = j % 2
                                        wat(te, "aex")
                                        if j == 0 and it >= 2:
                                            wat(te, "ayt", it - 1)
                                        inc(te.matmul(pU[0:65, ub * 512: ub * 512 + 512],
                                                      Var[:, j * 130 + h2 * 65: j * 130 + (h2 + 1) * 65],
                                                      Ebuf[:, ebj * 512:(ebj + 1) * 512],
                                                      start=(j == 0), stop=(j == nkc - 1),
                                                      skip_group_check=True),
                                            "apv", 1)

                                    for kc in range(nkc):
                                        eb = kc % 2
                                        if kc >= 2:
                                            wat(te, "aex", semc["aex"] - 1)
                                        delta2 = kc * 128 - tq0
                                        ins = te.matmul(
                                            pE[:, eb * 512: eb * 512 + 512],
                                            Kar[0:128, h2 * 4096 + kc * 128: h2 * 4096 + (kc + 1) * 128],
                                            Qar[0:128, hl * TH + qt * 512: hl * TH + (qt + 1) * 512],
                                            start=True, stop=(delta2 < 0), skip_group_check=True)
                                        if delta2 >= 0:
                                            ins = te.matmul(
                                                pE[:, eb * 512: eb * 512 + 512], identS[:, :],
                                                maskqkS[:, (delta2 // 128) * 512:
                                                        (delta2 // 128 + 1) * 512],
                                                start=False, stop=True, skip_group_check=True)
                                        inc(ins, "aqk", 1)
                                        if kc >= 1:
                                            emit_pv(kc - 1)
                                        wat(sc, "aqk")
                                        if kc >= 2:
                                            wat(sc, "apv", semc["apv"] - 1)
                                        inc(sc.activation(out=Ebuf[:, eb * 512:(eb + 1) * 512],
                                                          in_=pE[:, eb * 512: eb * 512 + 512],
                                                          func=AF.Exp), "aex", 1)
                                    emit_pv(nkc - 1)
                                    wat(ve, "acm"); wat(ve, "apv")
                                    ve.tensor_tensor(drS[64:65, 0:512],
                                                     pU[64:65, ub * 512: ub * 512 + 512],
                                                     cemS[64:65, 0:512], ALU.add)
                                    vsync()
                                    inc(ve.reciprocal(cemS[64:65, 0:512], drS[64:65, 0:512]),
                                        "ard", 1)
                                    pend[0] = (hl, qt, ub)
                                    itc[0] += 1
                            flush_pending()
                            wat(te, "ayt"); wat(sc, "ayt"); wat(sy, "ayt")
                            wat(gp, "ayt")
                            if stop_after == "aat" and g == 0:
                                incd_outg(gp.dma_start(out=out_d[0:64, 0:4096], in_=yTar[:, :]))
                                wat(gp, "outg")
                                return
                        barrier()
                        if stop_after in ("akv", "aat"):
                            continue
                        esw = ExitStack()
                        with esw:
                            wo64gS = esw.enter_context(
                                nc.sbuf_tensor(gn + "wo64gS", [64, 2 * 1024], BF16))
                            xst3 = esw.enter_context(
                                nc.sbuf_tensor(gn + "xst3", [128, 2 * 1024], F32))
                            pWO = esw.enter_context(nc.psum_tensor(gn + "pWO", [128, 2048], F32))
                            incd(gp.dma_start(out=wo64gS[:, :],
                                              in_=wo64_d[:, 2 * g * 1024:(2 * g + 2) * 1024]),
                                 "awg")
                            wat(te, "awg"); wat(te, "aws")
                            x3b = semc.get("ax", 0)
                            for qb in range(NB):
                                if g == 0:
                                    if qb >= 2:
                                        wat(sy, "ari", semc["ari"] - 1)
                                    incd(sy.dma_start(
                                        out=xst3[:, (qb % 2) * 1024:(qb % 2 + 1) * 1024],
                                        in_=xb_d[q0 + qb * 128: q0 + (qb + 1) * 128, :]),
                                        "ax")
                                if qb >= 2:
                                    wat(te, "ari", semc["ari"] - 1)
                                for hl in range(2):
                                    for eh in range(2):
                                        ins = te.matmul(
                                            pWO[:, (qb % 2) * 1024 + eh * 512:
                                                (qb % 2) * 1024 + (eh + 1) * 512],
                                            yTar[0:64, hl * TH + qb * 128: hl * TH + (qb + 1) * 128],
                                            wo64gS[0:64, hl * 1024 + eh * 512: hl * 1024 + (eh + 1) * 512],
                                            start=(hl == 0), stop=(hl == 1))
                                inc(ins, "awo", 1)
                                wat(ve, "awo")
                                if g == 0:
                                    wat(ve, "ax", x3b + 16 * (qb + 1))
                                    inc(ve.tensor_add(R[:, qb * EM:(qb + 1) * EM],
                                                      pWO[:, (qb % 2) * 1024:(qb % 2 + 1) * 1024],
                                                      xst3[:, (qb % 2) * 1024:(qb % 2 + 1) * 1024]),
                                        "ari", 1)
                                else:
                                    inc(ve.tensor_add(R[:, qb * EM:(qb + 1) * EM],
                                                      R[:, qb * EM:(qb + 1) * EM],
                                                      pWO[:, (qb % 2) * 1024:(qb % 2 + 1) * 1024]),
                                        "ari", 1)
                            wat(te, "ari")
                        barrier()
                if stop_after not in ("a1", "aqp", "akv", "aat"):
                    for eng in ENG:
                        wat(eng, "ari")

        def emit_attn_b():
            p = "c"
            esc = ExitStack()
            with esc:
                S2 = lambda n, s, d=F32: esc.enter_context(nc.sbuf_tensor(p + n, s, d))
                lngS = S2("lngS", [128, 8]); lnbS = S2("lnbS", [128, 8])
                bgateS = S2("bgateS", [16, 2]); eigS = S2("eigS", [16, 2])
                bm16S = S2("bm16S", [16, 8 * 128])
                in128S = S2("in128S", [128, TH])
                xpn = S2("xpn", [128, 8 * TH], BF16)
                hab = S2("hab", [16, 2 * TH])
                alpha = hab.ap()[0:16, 0:TH]; beta = hab.ap()[0:16, TH:2 * TH]
                in16 = in128S.ap()[0:16, :]
                incd(sy.dma_start(out=lngS[:, :], in_=lng_d), "cw")
                incd(sy.dma_start(out=lnbS[:, :], in_=lnb_d), "cw")
                incd(sy.dma_start(out=bgateS[:, :], in_=bgate_d), "cw")
                incd(sy.dma_start(out=eigS[:, 0:1], in_=eigr_d), "cw")
                incd(sy.dma_start(out=bm16S[:, :], in_=bm16_d), "cw")
                incd(sy.dma_start(out=in128S[:, :], in_=in128_d), "cw")

                es12 = ExitStack()
                with es12:
                    S3 = lambda n, s, d=F32: es12.enter_context(nc.sbuf_tensor(p + n, s, d))
                    xTb = S3("xTb", [128, 8 * TH], BF16)
                    winS = S3("winS", [128, 8 * 1024], BF16)
                    wgateS = S3("wgateS", [128, 8 * 32], BF16)
                    incd(gp.dma_start(out=winS[:, :], in_=win_d), "cwg")
                    incd(gp.dma_start(out=wgateS[:, :], in_=wgate_d), "cwg")
                    es1 = ExitStack()
                    with es1:
                        pT = es1.enter_context(nc.psum_tensor("cpT", [128, 2048], BF16))
                        xlnb = es1.enter_context(nc.sbuf_tensor("cxlnb", [128, 2048], BF16))
                        emit_ln_R_to_xTb(pT, xlnb, xTb)
                    barrier()
                    es2 = ExitStack()
                    with es2:
                        P2 = lambda n, s: es2.enter_context(nc.psum_tensor(p + n, s, F32))
                        pC1 = P2("pC1", [128, 1024])
                        pSt = P2("pSt", [128, 1024])
                        pBC2 = P2("pBC2", [128, 1024])
                        pGA = P2("pGA", [128, 1024])
                        xpsl = es2.enter_context(nc.sbuf_tensor("cxpsl", [128, 8 * 512], F32))
                        sqsl = es2.enter_context(nc.sbuf_tensor("csqsl", [128, 512], F32))
                        strow = es2.enter_context(nc.sbuf_tensor("cstrow", [1, 2 * 512], F32))
                        wat(te, "cwg"); wat(te, "cw"); wat(te, "cl")
                        for ts in range(4):
                            for mc in range(8):
                                if mc >= 2:
                                    wat(te, "cwiv", semc["cwiv"] - 1)
                                for hc in range(8):
                                    ins = te.matmul(
                                        pC1[:, (mc % 2) * 512:(mc % 2 + 1) * 512],
                                        winS[:, hc * 1024 + mc * 128: hc * 1024 + (mc + 1) * 128],
                                        xTb[:, hc * TH + ts * 512: hc * TH + (ts + 1) * 512],
                                        start=(hc == 0), stop=(hc == 7))
                                inc(ins, "cwi", 1)
                                wat(ve, "cwi")
                                inc(ve.tensor_copy(xpsl[:, mc * 512:(mc + 1) * 512],
                                                   pC1[:, (mc % 2) * 512:(mc % 2 + 1) * 512]),
                                    "cwiv", 1)
                            wat(te, "cwiv")
                            if ts >= 1:
                                wat(te, "cvr", semc["cvr"])
                            for hc in range(8):
                                inc(te.matmul(pSt[0:1, 0:512], onescolS[:, :],
                                              xpsl[:, hc * 512:(hc + 1) * 512],
                                              start=(hc == 0), stop=(hc == 7)), "cs1", 1)
                            wat(sc, "cwiv")
                            for hc in range(8):
                                if hc >= 1:
                                    wat(sc, "cs2", semc["cs2"])
                                inc(sc.activation(out=sqsl[:, :],
                                                  in_=xpsl[:, hc * 512:(hc + 1) * 512],
                                                  func=AF.Square), "csq", 1)
                                wat(te, "csq")
                                inc(te.matmul(pSt[0:1, 512:1024], onescolS[:, :], sqsl[:, :],
                                              start=(hc == 0), stop=(hc == 7)), "cs2", 1)
                            wat(ve, "cs1"); wat(ve, "cs2")
                            ve.tensor_scalar_mul(strow[0:1, 0:512], pSt[0:1, 0:512], 1.0 / EM)
                            ve.tensor_scalar_mul(strow[0:1, 512:1024], pSt[0:1, 512:1024],
                                                 1.0 / EM)
                            vsync()
                            ve.tensor_tensor(sqsl[0:1, 0:512], strow[0:1, 0:512],
                                             strow[0:1, 0:512], ALU.mult)
                            vsync()
                            inc(ve.tensor_sub(strow[0:1, 512:1024], strow[0:1, 512:1024],
                                              sqsl[0:1, 0:512]), "cvr", 1)
                            wat(sc, "cvr")
                            ssync(sc.activation(out=strow[0:1, 512:1024], in_=strow[0:1, 512:1024],
                                                func=AF.Ln, bias=actc[0:1, 0:1]))
                            inc(sc.activation(out=strow[0:1, 512:1024], in_=strow[0:1, 512:1024],
                                              func=AF.Exp, scale=actc[0:1, 1:2]), "crs", 1)
                            wat(ve, "crs")
                            inc(ve.scalar_tensor_tensor(strow[0:1, 0:512], strow[0:1, 0:512],
                                                        -1.0, strow[0:1, 512:1024],
                                                        ALU.mult, ALU.mult), "cmr", 1)
                            wat(te, "cmr")
                            if ts >= 1:
                                wat(te, "cnx", semc["cnx"])
                            te.matmul(pBC2[:, 0:512], onescS[:, :], strow[0:1, 512:1024],
                                      start=True, stop=True)
                            inc(te.matmul(pBC2[:, 512:1024], onescS[:, :], strow[0:1, 0:512],
                                          start=True, stop=True), "cbc2", 1)
                            wat(ve, "cbc2")
                            for hc in range(8):
                                ve.tensor_tensor(xpsl[:, hc * 512:(hc + 1) * 512],
                                                 xpsl[:, hc * 512:(hc + 1) * 512],
                                                 pBC2[:, 0:512], ALU.mult)
                                vsync()
                                inc(ve.tensor_tensor(xpsl[:, hc * 512:(hc + 1) * 512],
                                                     xpsl[:, hc * 512:(hc + 1) * 512],
                                                     pBC2[:, 512:1024], ALU.add), "cnx", 1)
                                wat(sc, "cnx")
                                inc(sc.activation(out=sqsl[:, 0:512],
                                                  in_=xpsl[:, hc * 512:(hc + 1) * 512],
                                                  func=AF.Copy, scale=lngS[:, hc:hc + 1]),
                                    "cxpc", 1)
                                wat(ve, "cxpc")
                                inc(ve.tensor_scalar_add(
                                    xpn[:, hc * TH + ts * 512: hc * TH + (ts + 1) * 512],
                                    sqsl[:, 0:512], lnbS[:, hc:hc + 1]), "cxp", 1)
                            wat(te, "cxp")
                            if ts >= 1:
                                wat(te, "cab", semc["cab"])
                            for hc in range(8):
                                te.matmul(pGA[0:16, 0:512], wgateS[:, hc * 32: hc * 32 + 16],
                                          xpn[:, hc * TH + ts * 512: hc * TH + (ts + 1) * 512],
                                          start=(hc == 0), stop=(hc == 7))
                            for hc in range(8):
                                ins2 = te.matmul(pGA[0:16, 512:1024],
                                                 wgateS[:, hc * 32 + 16: hc * 32 + 32],
                                                 xpn[:, hc * TH + ts * 512: hc * TH + (ts + 1) * 512],
                                                 start=(hc == 0), stop=(hc == 7))
                            inc(ins2, "cga", 1)
                            wat(sc, "cga")
                            sc.activation(out=alpha[:, ts * 512:(ts + 1) * 512],
                                          in_=pGA[0:16, 0:512], func=AF.Sigmoid,
                                          bias=bgateS[0:16, 0:1])
                            inc(sc.activation(out=beta[:, ts * 512:(ts + 1) * 512],
                                              in_=pGA[0:16, 512:1024], func=AF.Sigmoid,
                                              bias=bgateS[0:16, 1:2]), "cab", 1)
                if stop_after == "xpn":
                    wat(gp, "cxp")
                    incd_outg(gp.dma_start(out=out_d, in_=xpn[:, :]))
                    wat(gp, "outg")
                    return
                barrier()
                esL = esc.enter_context(ExitStack())
                SL = lambda n, s, d=F32: esL.enter_context(nc.sbuf_tensor(p + n, s, d))
                hcr = SL("hcr", [16, 2 * TH])
                hw = SL("hw", [16, 2 * TH])
                rbS = SL("rbS", [128, TH]); Ssc = SL("Ssc", [128, TH])
                cnar = SL("cnar", [128, 8 * 128]); enar = SL("enar", [128, 8 * 128])
                hst = SL("hst", [128, 128]); hprev = SL("hprev", [128, 128])
                hlocar = SL("hlocar", [128, 8 * 128])
                init8 = SL("init8", [128, 8]); sndS = SL("sndS", [128, 16])
                rcvS = SL("rcvS", [128, 16])
                tmpa = SL("tmpa", [128, 512]); tmpb = SL("tmpb", [128, 512])
                cum16 = hcr.ap()[0:16, 0:TH]; agb = hcr.ap()[0:16, TH:2 * TH]
                tmpg = rbS.ap()
                w0 = hw.ap()[0:16, 0:TH]; w1 = hw.ap()[0:16, TH:2 * TH]
                d0S = SL("d0S", [128, TH])
                inc(sc.activation(out=eigS[0:16, 1:2], in_=eigS[0:16, 0:1], func=AF.Tanh),
                    "ceg", 1)
                wat(ve, "ceg"); wat(ve, "cab"); wat(ve, "cw"); wat(ve, "cwg")
                inc(ve.tensor_copy(smll[:, 28:29], smll[:, 29:30]), "cegv", 1)
                wat(ve, "cegv")
                ve.tensor_scalar_mul(alpha, alpha, eigS[0:16, 1:2])
                vsync()
                ve.tensor_tensor(w0, alpha, in16, ALU.mult)
                vsync()
                ve.tensor_sub(w1, alpha, w0)
                vheal()
                ve.tensor_tensor_scan(cum16, w0, w1, 0.0, ALU.mult, ALU.add)
                vsync()
                ve.tensor_scalar(w1, in16, -1.0, 1.0, ALU.mult, ALU.add)
                ve.tensor_tensor(tmpg[0:16, 1:TH], cum16[:, 0:TH - 1], in16[:, 1:TH], ALU.mult)
                ve.memset(tmpg[0:16, 0:1], 0.0)
                vsync()
                ve.tensor_add(w1, w1, tmpg[0:16, 0:TH])
                vsync()
                ve.tensor_single_scalar(w1, w1, 1e-8, ALU.is_gt)
                ve.tensor_tensor(agb, alpha, beta, ALU.mult)
                vsync()
                inc(ve.tensor_tensor(agb, agb, w1, ALU.mult), "chl", 1)

                st15 = lambda ap: ap.rearrange("a (n f) -> a n f", f=16)[:, :, 15:16].rearrange(
                    "a n f -> a (n f)")
                st14 = lambda ap: ap.rearrange("a (n f) -> a n f", f=16)[:, :, 14:15].rearrange(
                    "a n f -> a (n f)")

                es3 = ExitStack()
                with es3:
                    pB3 = es3.enter_context(nc.psum_tensor("cpB3", [128, 1536], F32))

                    def emit_rb_scan(hp):
                        wat(te, "cha"); wat(te, "cenv"); wat(te, "csn")
                        for tt in range(4):
                            cols = slice(tt * 512, (tt + 1) * 512)
                            wat(te, "chl")
                            if semc.get("cb3", 0) >= 1:
                                wat(te, "crb", semc["crb"])
                            te.matmul(pB3[:, 1024:1536],
                                      bm16S[:, hp * 128:(hp + 1) * 128],
                                      agb[:, cols], start=True, stop=True)
                            inc(te.matmul(pB3[:, 512:1024],
                                          bm16S[:, hp * 128:(hp + 1) * 128],
                                          w0[:, cols], start=True, stop=True), "cb3", 1)
                            wat(ve, "cb3")
                            ve.tensor_tensor(rbS[:, cols],
                                             xpn[:, hp * TH + tt * 512: hp * TH + (tt + 1) * 512],
                                             pB3[:, 1024:1536], ALU.mult)
                            inc(ve.tensor_copy(d0S[:, cols], pB3[:, 512:1024]), "crb", 1)
                        vheal()
                        inc(ve.tensor_tensor_scan(Ssc[:, :], d0S[:, :], rbS[:, :],
                                                  0.0, ALU.mult, ALU.add), "css", 1)

                    for hp in range(8):
                        ccols = slice(hp * 128, (hp + 1) * 128)
                        emit_rb_scan(hp)
                        wat(te, "css")
                        if semc.get("ccnv", 0) >= 1:
                            wat(te, "cenv", semc["cenv"])
                        te.matmul(pB3[:, 0:128], bm16S[:, hp * 128:(hp + 1) * 128],
                                  st15(cum16), start=True, stop=True)
                        inc(te.matmul(pB3[:, 128:256], bm16S[:, hp * 128:(hp + 1) * 128],
                                      st15(beta), start=True, stop=True), "ccb", 1)
                        wat(ve, "ccb")
                        inc(ve.tensor_copy(cnar.ap()[:, ccols], pB3[:, 0:128]), "ccnv", 1)
                        vsync()
                        ve.tensor_tensor(enar.ap()[:, ccols], st15(d0S.ap()[:, :]),
                                         st14(Ssc.ap()[:, :]), ALU.mult)
                        ve.tensor_tensor(tmpa[:, 0:128],
                                         st15(xpn.ap()[:, hp * TH:(hp + 1) * TH]),
                                         pB3[:, 128:256], ALU.mult)
                        vsync()
                        inc(ve.tensor_add(enar.ap()[:, ccols], enar.ap()[:, ccols],
                                          tmpa[:, 0:128]), "cenv", 1)
                        vheal()
                        inc(ve.tensor_tensor_scan(hlocar.ap()[:, ccols], cnar.ap()[:, ccols],
                                                  enar.ap()[:, ccols], 0.0,
                                                  ALU.mult, ALU.add), "csn", 1)
                    wat(ve, "csn"); wat(ve, "cl")
                    ve.tensor_scalar(smll[:, 17:18], flagS[:, 0:1], -1.0, 1.0, ALU.mult, ALU.add)
                    vheal()
                    ve.tensor_scalar_mul(sndS[:, 8:16], hlocar.ap().rearrange("a (n f) -> a n f", f=128)[:, :, 127:128].rearrange("a n f -> a (n f)"), flagS[:, 0:1])
                    vsync()
                    inc(ve.tensor_scalar_mul(sndS[:, 0:8], hlocar.ap().rearrange("a (n f) -> a n f", f=128)[:, :, 127:128].rearrange("a n f -> a (n f)"), smll[:, 17:18]),
                        "csf", 1)
                    wat(gp, "csf")
                    incd(gp.dma_start(out=snd_d, in_=sndS[:, :]), "csd")
                    wat(gp, "csd")
                    inc(gp.collective_compute("AllReduce", ALU.add,
                                              replica_groups=[[0, 1], [2, 3], [4, 5], [6, 7]],
                                              ins=[snd_d.opt()], outs=[rcv_d.opt()]), "ccc", 1)
                    wat(gp, "ccc")
                    incd(gp.dma_start(out=rcvS[:, :], in_=rcv_d), "crv")
                    wat(ve, "crv")
                    inc(ve.tensor_scalar_mul(init8[:, :], rcvS[:, 0:8], flagS[:, 0:1]),
                        "ci8", 1)
                    if stop_after == "exch":
                        wat(gp, "ci8")
                        incd_outg(gp.dma_start(out=out_d[:, 0:16], in_=sndS[:, :]))
                        incd_outg(gp.dma_start(out=out_d[:, 16:32], in_=rcvS[:, :]))
                        incd_outg(gp.dma_start(out=out_d[:, 32:40], in_=init8[:, :]))
                        incd_outg(gp.dma_start(out=out_d[:, 64:64 + 1024], in_=cnar[:, :]))
                        incd_outg(gp.dma_start(out=out_d[:, 2048:2048 + 1024], in_=enar[:, :]))
                        incd_outg(gp.dma_start(out=out_d[:, 3200:3200 + 128], in_=hst[:, :]))
                        wat(gp, "outg")
                        return
                    for hp in range(8):
                        ccols = slice(hp * 128, (hp + 1) * 128)
                        wat(ve, "ci8")
                        vheal()
                        ve.tensor_tensor_scan(hst[:, :], cnar.ap()[:, ccols],
                                              enar.ap()[:, ccols], init8[:, hp:hp + 1],
                                              ALU.mult, ALU.add)
                        vsync()
                        ve.tensor_copy(hprev[:, 1:128], hst[:, 0:127])
                        inc(ve.tensor_copy(hprev[:, 0:1], init8[:, hp:hp + 1]), "chp", 1)
                        emit_rb_scan(hp)
                        wat(ve, "css")
                        for tt in range(4):
                            cols = slice(tt * 512, (tt + 1) * 512)
                            wat(te, "chp")
                            if semc.get("cb4", 0) >= 1:
                                wat(te, "cha", semc["cha"])
                            te.matmul(pB3[:, 0:512], bm16S[:, hp * 128:(hp + 1) * 128],
                                      cum16[:, cols], start=True, stop=True)
                            inc(te.matmul(pB3[:, 512:1024], bm16S[:, hp * 128:(hp + 1) * 128],
                                          beta[:, cols], start=True, stop=True), "cb4", 1)
                            wat(ve, "cb4")
                            if tt == 0:
                                ve.memset(tmpa[:, 0:1], 0.0)
                                ve.tensor_tensor(tmpa[:, 1:512], Ssc[:, 0:511],
                                                 d0S[:, 1:512], ALU.mult)
                            else:
                                ve.tensor_tensor(tmpa[:, 0:512],
                                                 Ssc[:, tt * 512 - 1:(tt + 1) * 512 - 1],
                                                 d0S[:, tt * 512:(tt + 1) * 512], ALU.mult)
                            hpb = hprev[:, tt * 32:(tt + 1) * 32].rearrange(
                                "a (n r) -> a n r", r=1).broadcast_to((128, 32, 16))
                            ve.tensor_tensor(tmpb[:, 0:512].rearrange("a (n f) -> a n f", f=16),
                                             pB3[:, 0:512].rearrange("a (n f) -> a n f", f=16),
                                             hpb, ALU.mult)
                            vsync()
                            ve.tensor_add(tmpa[:, 0:512], tmpa[:, 0:512], tmpb[:, 0:512])
                            ve.tensor_tensor(tmpb[:, 0:512],
                                             xpn[:, hp * TH + tt * 512: hp * TH + (tt + 1) * 512],
                                             pB3[:, 512:1024], ALU.mult)
                            vsync()
                            inc(ve.tensor_add(
                                xpn[:, hp * TH + tt * 512: hp * TH + (tt + 1) * 512],
                                tmpa[:, 0:512], tmpb[:, 0:512]), "cha", 1)
                barrier()
                esL.close()
                es4 = ExitStack()
                with es4:
                    woutS = es4.enter_context(nc.sbuf_tensor("cwoutS", [128, 8 * 1024], BF16))
                    pWo = es4.enter_context(nc.psum_tensor("cpWo", [128, 2048], F32))
                    incd(gp.dma_start(out=woutS[:, :], in_=wout_d), "cwg4")
                    wat(te, "cwg4")
                    for qb in range(NB):
                        if qb >= 2:
                            wat(te, "cra", semc["cra"] - 1)
                        for hp in range(8):
                            for eh in range(2):
                                ins = te.matmul(
                                    pWo[:, (qb % 2) * 1024 + eh * 512:(qb % 2) * 1024 + (eh + 1) * 512],
                                    xpn[:, hp * TH + qb * 128: hp * TH + (qb + 1) * 128],
                                    woutS[:, hp * 1024 + eh * 512: hp * 1024 + (eh + 1) * 512],
                                    start=(hp == 0), stop=(hp == 7))
                        inc(ins, "cwo2", 1)
                        wat(ve, "cwo2")
                        inc(ve.tensor_add(R[:, qb * EM:(qb + 1) * EM],
                                          R[:, qb * EM:(qb + 1) * EM],
                                          pWo[:, (qb % 2) * 1024:(qb % 2 + 1) * 1024]),
                            "cra", 1)
            barrier()

        for eng in ENG:
            wat(eng, "cl")
        if do_a and noswitch:
            emit_attn(0, 0, TH)
            if stop_after in ("a1", "aqp", "akv", "aat"):
                return nc
            barrier()
        elif do_a:
            snap = dict(semc)
            idxs = [eng.partition_id() % 2 for eng in ENG]
            CROSS = ("vv", "ssy", "Lln", "Lv", "Ls", "Lv2", "Ls2", "Lv3",
                     "Lcp", "Ltp", "Ltv", "bar")
            cnt = [None, None]
            for i in nc.Switch(engines=ENG, index=idxs, n=2):
                for k in list(semc.keys()):
                    semc[k] = snap.get(k, 0)
                if i == 0:
                    emit_attn(0, 0, TH)
                else:
                    emit_attn(1, TH, T)
                cnt[i] = {n: semc.get(n, 0) for n in CROSS}
            if stop_after in ("a1", "aqp", "akv", "aat"):
                return nc
            for i in nc.Switch(engines=ENG, index=idxs, n=2):
                if i == 0:
                    for n in CROSS:
                        delta = cnt[1][n] - cnt[0][n]
                        assert delta >= 0, (n, delta)
                        for _ in range(delta):
                            te.nop().then_inc(sem(n), 1)
                else:
                    te.nop()
            barrier()
        else:
            for qb in range(NB):
                incd(sy.dma_start(out=R[:, qb * EM:(qb + 1) * EM],
                                  in_=xb_d[qb * 128:(qb + 1) * 128, :]), "r0")
            for eng in ENG:
                wat(eng, "r0")
            barrier()
        if do_b:
            emit_ffn("B")
            if stop_after in ("ln", "router"):
                return nc
        if do_c:
            emit_attn_b()
            if stop_after in ("xpn", "exch"):
                return nc
        if do_d:
            emit_ffn("D")
        incd(sy.dma_start(out=out_d, in_=R[:, :]), "out")
        wat(sy, "out")
    return nc


_CACHE = {}

def kernel(**inputs):
    ii = {k: np.asarray(v) for k, v in inputs.items()}
    x = ii["L0_attn_a_x"].astype(np.float32)
    consts = host_consts()
    shared = {
        "wq": pack_w(ii["L0_attn_a_Wq"]),
        "wqr": pack_w(ii["L0_attn_a_Wqr"]),
        "wkr": pack_w(ii["L0_attn_a_Wkr"]),
        "wkvd": pack_w(ii["L0_attn_a_Wkvd"]),
        "wku": pack_w(ii["L0_attn_a_Wku"]),
        "wvu": pack_w(ii["L0_attn_a_Wvu"]),
        "wo64": pack_w64(ii["L0_attn_a_Wo"]),
        "lobo": ii["L0_attn_a_lobo_log"].reshape(1, 16).astype(np.float32),
        "win": pack_w(np.ascontiguousarray(ii["L0_attn_b_W_in"].T)),
        "wgate": pack_w(np.ascontiguousarray(ii["L0_attn_b_W_gate"].T)),
        "wout": pack_w(np.ascontiguousarray(ii["L0_attn_b_W_out"].T)),
        "lng": np.ascontiguousarray(ii["L0_attn_b_ln_g"].reshape(8, 128).T.astype(np.float32)),
        "lnb": np.ascontiguousarray(ii["L0_attn_b_ln_b"].reshape(8, 128).T.astype(np.float32)),
        "bgate": np.ascontiguousarray(ii["L0_attn_b_b_gate"].reshape(2, 16).T.astype(np.float32)),
        "eigr": ii["L0_attn_b_eig_raw"].reshape(16, 1).astype(np.float32),
        "wrt": pack_w(ii["L0_ffn_router_w"]),
        "sgw": pack_w(ii["L0_ffn_shared_gate_w"]),
        "suw": pack_w(ii["L0_ffn_shared_up_w"]),
        "sdw": pack_w(ii["L0_ffn_shared_down_w"]),
        "weg": np.stack([pack_w(ii["L0_ffn_expert_gate_k"][e]) for e in range(NEXP)]),
        "weu": np.stack([pack_w(ii["L0_ffn_expert_up_k"][e]) for e in range(NEXP)]),
        "wed": np.stack([pack_w(ii["L0_ffn_expert_down_k"][e]) for e in range(NEXP)]),
    }
    shared.update(consts)
    if "nc" not in _CACHE:
        _CACHE["nc"] = build_nc()
    nc = _CACHE["nc"]
    in_maps = []
    for c in range(8):
        b, half = c // 2, c % 2
        m = dict(shared)
        m["xb"] = np.ascontiguousarray(x[b])
        fl = np.zeros((128, 2), np.float32)
        fl[:, 0] = float(half)
        m["flags"] = fl
        in_maps.append(m)
    res = run_bass_kernel_spmd(nc, in_maps, core_ids=list(range(8)))
    out = np.empty((4, 4096, 1024), np.float32)
    for c in range(8):
        b, half = c // 2, c % 2
        out[b, half * TH:(half + 1) * TH, :] = unblockify(res.results[c]["out"], EM)
    return out
